# version 02
# baseline (speedup 1.0000x reference)
import numpy as np
from contextlib import ExitStack
import concourse.bass as bass
import concourse.mybir as mybir
from concourse.bass_utils import run_bass_kernel_spmd

F32 = mybir.dt.float32
BF16 = mybir.dt.bfloat16
AF = mybir.ActivationFunctionType
ALU = mybir.AluOpType
AX = mybir.AxisListType

NCORES = 8
T = 4096
TE = 4608
HB = 3
D = 1024
Q = 128
NCH = T // Q


class P:

    def __init__(self, name):
        self.name = name
        self.nc = bass.Bass("TRN2", target_bir_lowering=False, debug=False)
        self.stack = ExitStack()
        self.ops = []
        self.n_sb = 0
        self.n_scr = 0

    def inp(self, name, shape, dtype=F32):
        return self.nc.dram_tensor(name, list(shape), dtype, kind="ExternalInput").ap()

    def out(self, name, shape, dtype=F32):
        return self.nc.dram_tensor(name, list(shape), dtype, kind="ExternalOutput").ap()

    def dram(self, name, shape, dtype=F32):
        return self.nc.dram_tensor(name, list(shape), dtype, kind="Internal").ap()

    def sb(self, shape, dtype=F32, name=None):
        self.n_sb += 1
        return self.stack.enter_context(
            self.nc.sbuf_tensor(name or f"sb{self.n_sb}", list(shape), dtype))

    def sb2(self, shape, dtype=F32, name=None):
        self.n_sb += 1
        nm = name or f"sb{self.n_sb}"
        cm = self.nc.sbuf_tensor(nm, list(shape), dtype)
        t = cm.__enter__()
        if not hasattr(self, "_cms"):
            self._cms = {}
        self._cms[nm] = cm
        return t

    def ps2(self, shape, dtype=F32, name=None):
        self.n_sb += 1
        nm = name or f"ps{self.n_sb}"
        cm = self.nc.psum_tensor(nm, list(shape), dtype)
        t = cm.__enter__()
        if not hasattr(self, "_cms"):
            self._cms = {}
        self._cms[nm] = cm
        return t

    def sb_free(self, *names):
        for nm in names:
            self._cms.pop(nm).__exit__(None, None, None)

    def sb_free_all(self):
        for nm in reversed(list(getattr(self, "_cms", {}))):
            self._cms.pop(nm).__exit__(None, None, None)

    def psum(self, shape, dtype=F32, name=None):
        self.n_sb += 1
        return self.stack.enter_context(
            self.nc.psum_tensor(name or f"ps{self.n_sb}", list(shape), dtype))

    def warm(self):
        w = self.sb2([128, 4], F32, "warmf")
        wb = self.sb2([128, 4], BF16, "warmb")
        for v in (0.0, 1.0):
            self.memset(w[:], v)
            self.memset(wb[:], v)

    def o(self, eng, fn, dma=False):
        self.ops.append((eng, dma, fn))

    def dma(self, out, in_, eng="sync", slow=False):
        if slow:
            self.o(eng, lambda e: e.dma_start(out=out, in_=in_,
                                              allow_slow_non_contiguous=True), dma=True)
        else:
            self.o(eng, lambda e: e.dma_start(out=out, in_=in_), dma=True)

    def cast_dma(self, out, in_):
        self.o("gpsimd", lambda e: e.dma_start(out=out, in_=in_), dma=True)

    def dmat(self, out, in_):
        self.o("sync", lambda e: e.dma_start_transpose(out, in_), dma=True)

    def mm(self, out, lhsT, rhs, start=True, stop=True):
        self.o("tensor", lambda e: e.matmul(out, lhsT, rhs, start=start, stop=stop))

    def act(self, out, in_, func, bias=None, scale=None, accum_out=None):
        kw = {}
        if bias is not None:
            kw["bias"] = bias
        if scale is not None:
            kw["scale"] = scale
        if accum_out is not None:
            kw["accum_out"] = accum_out
        self.o("scalar", lambda e: e.activation(out=out, in_=in_, func=func, **kw))

    def tt(self, out, a, b, op, eng="vector"):
        self.o(eng, lambda e: e.tensor_tensor(out, a, b, op))

    def ts(self, out, in_, s1, s2=None, op0=ALU.mult, op1=None, eng="vector"):
        if op1 is None:
            self.o(eng, lambda e: e.tensor_scalar(out, in_, s1, None, op0))
        else:
            self.o(eng, lambda e: e.tensor_scalar(out, in_, s1, s2, op0, op1))

    def stt(self, out, in0, scalar, in1, op0, op1, eng="vector"):
        self.o(eng, lambda e: e.scalar_tensor_tensor(out, in0, scalar, in1, op0, op1))

    def red(self, out, in_, op, absval=None, eng="vector"):
        self.o(eng, lambda e: e.tensor_reduce(out, in_, AX.X, op,
                                              apply_absolute_value=absval))

    def scan(self, out, a, b, init, op0=ALU.mult, op1=ALU.add):
        self.o("vector", lambda e: e.tensor_tensor_scan(out, a, b, init, op0, op1))

    def cp(self, out, in_, eng="vector"):
        self.o(eng, lambda e: e.tensor_copy(out, in_))

    def memset(self, ap, v, eng="vector"):
        self.o(eng, lambda e: e.memset(ap, v))

    def recip(self, out, in_):
        self.o("vector", lambda e: e.reciprocal(out, in_))

    def transp(self, out, in_, ident):
        self.o("tensor", lambda e: e.transpose(out, in_, ident))

    def softplus(self, out, in_, tmp):
        self.act(tmp, in_, AF.Abs)
        self.act(tmp, tmp, AF.Exp, scale=-1.0)
        self.act(tmp, tmp, AF.Ln, bias=1.0)
        self.act(out, in_, AF.Relu)
        self.tt(out, out, tmp, ALU.add)

    def bcast(self, dst, src_row, pn, n):
        self.n_scr += 1
        scr = self.dram(f"bscr{self.n_scr}", [1, n])
        self.dma(scr, src_row)
        self.dma(dst, scr.broadcast_to([pn, n]))

    def finalize(self):
        sem = self.stack.enter_context(self.nc.semaphore("gsem"))
        block = self.stack.enter_context(self.nc.Block())
        pres = []
        total = 0
        for eng, dmaf, fn in self.ops:
            pres.append(total)
            total += 16 if dmaf else 1
        self.total = total

        def run_engine(ename, e):
            for (eng, dmaf, fn), pre in zip(self.ops, pres):
                if eng != ename:
                    continue
                e.wait_ge(sem, pre)
                fn(e).then_inc(sem, 16 if dmaf else 1)
            if ename == "sync":
                e.wait_ge(sem, total)

        @block.sync
        def _(e):
            run_engine("sync", e)

        @block.gpsimd
        def _(e):
            run_engine("gpsimd", e)

        @block.vector
        def _(e):
            run_engine("vector", e)

        @block.scalar
        def _(e):
            run_engine("scalar", e)

        @block.tensor
        def _(e):
            run_engine("tensor", e)
        self.sb_free_all()
        self.stack.close()
        return self.nc


_prog_cache = {}


def run_prog(build_fn, in_maps):
    key = build_fn.__name__
    if key not in _prog_cache:
        _prog_cache[key] = build_fn().finalize()
    return run_bass_kernel_spmd(_prog_cache[key], in_maps,
                                core_ids=list(range(NCORES)))


def build_l0a():
    p = P("l0a")
    p.warm()
    x_ext = p.inp("x_ext", [1024, TE])
    w_inproj = p.inp("w_inproj", [1024, 4416])
    conv_w = p.inp("conv_w", [2304, 7])
    conv_b = p.inp("conv_b", [2304, 1])
    dtb = p.inp("dtb", [64, 1])
    alog = p.inp("alog", [64, 1])
    mu_w = p.inp("mu", [1024, 256])
    lngb = p.inp("lngb", [1024, 2])
    scale1 = p.inp("scale1", [1, 1])
    ident = p.inp("ident", [128, 128])

    o_siluz = p.out("siluz_bf", [2048, TE], BF16)
    o_xog = p.out("xog_bf", [2048, T], BF16)
    o_bc = p.out("bc_bf", [256, T], BF16)
    o_lcum = p.out("lcum", [64, T])
    o_expl = p.out("expl", [64, T])
    o_lendraw = p.out("lendraw", [64, NCH])
    o_lendexp = p.out("lendexp", [64, NCH])
    o_dte_bf = p.out("dte_bf", [64, T], BF16)
    o_sfin = p.out("sfin", [128, 2048])
    o_ybound = p.out("ybound", [2048, 2])
    o_E = p.out("ffn_E_bf", [256, TE], BF16)
    o_num = p.out("ffn_num", [1024, 256])
    o_sd = p.out("ffn_sd", [128, 2])
    o_scomb = p.out("ffn_scomb", [1, TE])

    d_xbc = p.dram("d_xbc", [2304, TE])
    d_xn2 = p.dram("d_xn2", [1024, TE], BF16)

    onesf = p.sb2([128, 8], F32, "onesf")
    xs = p.sb2([128, 512], F32, "xs")
    xsq = p.sb2([128, 512], F32, "xsq")
    bc1 = p.sb2([128, 512], F32, "bc1")
    bc2 = p.sb2([128, 512], F32, "bc2")
    ev = p.sb2([128, 512], F32, "ev")
    ev2 = p.sb2([128, 512], F32, "ev2")
    evb = p.sb2([128, 512], BF16, "evb")
    glob = p.sb2([1, 8], F32, "glob")
    negabs = p.sb2([128, 1], F32, "negabs")
    dte_res = p.sb2([64, TE], F32, "dte_res")
    r64a = p.sb2([64, T], F32, "r64a")
    r64b = p.sb2([64, T], F32, "r64b")
    lgb = p.sb2([128, 16], F32, "lgb")
    dtbs = p.sb2([64, 1], F32, "dtbs")
    a64 = p.sb2([64, 1], F32, "a64")
    zq = p.sb2([64, Q], F32, "zq")
    le = p.sb2([64, NCH], F32, "le")
    cw = p.sb2([128, 7 * 18], F32, "cw")
    cb = p.sb2([128, 18], F32, "cb")
    sdacc = p.sb2([128, 2], F32, "sdacc")
    mu_u = p.sb2([128, 8 * 256], BF16, "mu_u")
    muf = p.sb2([128, 256], F32, "muf")
    ps_m = p.psum([128, 512], F32, "ps_m")
    ps_r1 = p.psum([128, 512], F32, "ps_r1")
    ps_r2 = p.psum([128, 512], F32, "ps_r2")
    ps_big = p.psum([128, 2048], F32, "ps_big")

    p.memset(onesf[:], 1.0)
    p.memset(sdacc[:], 0.0)
    p.memset(zq[:], 0.0)
    p.dma(dtbs[:], dtb)
    for kc in range(8):
        p.dma(lgb[:, 2 * kc:2 * kc + 2], lngb[kc * 128:(kc + 1) * 128, :])
    for cc in range(18):
        p.dma(cw[:, cc * 7:(cc + 1) * 7], conv_w[cc * 128:(cc + 1) * 128, :])
        p.dma(cb[:, cc:cc + 1], conv_b[cc * 128:(cc + 1) * 128, :])
    p.dma(a64[:], alog)
    p.act(a64[:], a64[:], AF.Exp)
    p.ts(a64[:], a64[:], -1.0)
    p.dma(glob[0:1, 0:1], scale1)
    p.act(glob[0:1, 1:2], glob[0:1, 0:1], AF.Abs)
    p.ts(glob[0:1, 1:2], glob[0:1, 1:2], -1.0)
    p.bcast(negabs[:, 0:1], glob[0:1, 1:2], 128, 1)
    for kc in range(8):
        p.dma(muf[:], mu_w[kc * 128:(kc + 1) * 128, :])
        p.act(ev[:, 0:256], muf[:], AF.Square)
        p.mm(ps_r1[0:1, 0:256], onesf[:, 0:1], ev[:, 0:256],
             start=(kc == 0), stop=(kc == 7))
    murow = p.sb2([1, 3 * 256], F32, "murow")
    p.cp(murow[0:1, 0:256], ps_r1[0:1, 0:256])
    p.ts(murow[0:1, 0:256], murow[0:1, 0:256], 1e-24, None, ALU.max)
    p.act(murow[0:1, 256:512], murow[0:1, 0:256], AF.Sqrt)
    p.recip(murow[0:1, 512:768], murow[0:1, 256:512])
    p.bcast(bc1[:, 0:256], murow[0:1, 512:768], 128, 256)
    for kc in range(8):
        p.dma(muf[:], mu_w[kc * 128:(kc + 1) * 128, :])
        p.tt(mu_u[:, kc * 256:(kc + 1) * 256], muf[:], bc1[:, 0:256], ALU.mult)

    wi = p.sb2([128, 8 * 4416], BF16, "wi")
    xn_bf = p.sb2([128, 8 * 512], BF16, "xn_bf")
    xn2f = p.sb2([128, 512], F32, "xn2f")
    xn2b = p.sb2([128, 8 * 512], BF16, "xn2b")
    e_f = p.sb2([128, 2 * 512], F32, "e_f")
    e_b = p.sb2([128, 2 * 512], BF16, "e_b")
    rows_t = p.sb2([1, 8 * 512], F32, "rows_t")
    for kc in range(8):
        p.cast_dma(wi[:, kc * 4416:(kc + 1) * 4416],
                   w_inproj[kc * 128:(kc + 1) * 128, :])

    _rmap = {0: 0, 1: 1, 2: 2, 3: 3, 4: 0, 5: 1, 6: 4, 7: 7,
             8: 0, 9: 4, 10: 0, 11: 4, 12: 5, 13: 6, 14: 6, 15: 6}

    def r_(i):
        i = _rmap[i]
        return rows_t[0:1, i * 512:(i + 1) * 512]
    for it in range(TE // 512):
        c0 = it * 512
        for kc in range(8):
            p.dma(xs[:], x_ext[kc * 128:(kc + 1) * 128, c0:c0 + 512])
            p.act(xsq[:], xs[:], AF.Square)
            p.mm(ps_r1[0:1, 0:512], onesf[:, 0:1], xs[:],
                 start=(kc == 0), stop=(kc == 7))
            p.mm(ps_r2[0:1, 0:512], onesf[:, 1:2], xsq[:],
                 start=(kc == 0), stop=(kc == 7))
        p.cp(r_(0), ps_r1[0:1, 0:512])
        p.cp(r_(1), ps_r2[0:1, 0:512])
        p.ts(r_(2), r_(0), 1.0 / D)
        p.act(r_(4), r_(2), AF.Square)
        p.ts(r_(5), r_(1), 1.0 / D)
        p.tt(r_(5), r_(5), r_(4), ALU.subtract)
        p.ts(r_(6), r_(5), 1e-5, None, ALU.add)
        p.act(r_(7), r_(6), AF.Sqrt)
        p.recip(r_(3), r_(7))
        p.recip(r_(8), r_(6))
        p.tt(r_(9), r_(5), r_(8), ALU.mult)
        p.ts(r_(9), r_(9), 1e-5, None, ALU.add)
        p.act(r_(10), r_(9), AF.Sqrt)
        p.recip(r_(11), r_(10))
        p.tt(r_(12), r_(3), r_(11), ALU.mult)
        p.bcast(bc1[:], r_(2), 128, 512)
        p.bcast(bc2[:], r_(3), 128, 512)
        for kc in range(8):
            p.dma(xs[:], x_ext[kc * 128:(kc + 1) * 128, c0:c0 + 512])
            p.tt(xsq[:], xs[:], bc1[:], ALU.subtract)
            p.tt(xn_bf[:, kc * 512:(kc + 1) * 512], xsq[:], bc2[:], ALU.mult)
        p.bcast(bc2[:], r_(12), 128, 512)
        for kc in range(8):
            p.dma(xs[:], x_ext[kc * 128:(kc + 1) * 128, c0:c0 + 512])
            p.tt(xsq[:], xs[:], bc1[:], ALU.subtract)
            p.tt(xsq[:], xsq[:], bc2[:], ALU.mult)
            p.ts(xn2f[:], xsq[:], lgb[:, 2 * kc:2 * kc + 1],
                 lgb[:, 2 * kc + 1:2 * kc + 2], ALU.mult, ALU.add)
            p.cp(xn2b[:, kc * 512:(kc + 1) * 512], xn2f[:])
            p.act(xsq[:], xn2f[:], AF.Square)
            p.mm(ps_r1[0:1, 0:512], onesf[:, 2:3], xsq[:],
                 start=(kc == 0), stop=(kc == 7))
        p.cp(r_(13), ps_r1[0:1, 0:512])
        p.dma(d_xn2[:, c0:c0 + 512].rearrange("(a p) k -> p a k", p=128),
              xn2b[:].rearrange("p (a k) -> p a k", a=8), eng="gpsimd")
        p.act(r_(14), r_(13), AF.Sqrt)
        p.ts(r_(14), r_(14), 1e-12, None, ALU.max)
        p.recip(r_(15), r_(14))
        p.ts(r_(15), r_(15), glob[0:1, 0:1], None, ALU.mult)
        for mc in range(35):
            m0 = mc * 128
            msz = min(128, 4416 - m0)
            for kc in range(8):
                p.mm(ps_m[0:msz, 0:512],
                     wi[:, kc * 4416 + m0:kc * 4416 + m0 + msz],
                     xn_bf[:, kc * 512:(kc + 1) * 512],
                     start=(kc == 0), stop=(kc == 7))
            if m0 + msz <= 2048:
                p.act(ev[0:msz, :], ps_m[0:msz, 0:512], AF.Silu)
                p.cp(evb[0:msz, :], ev[0:msz, :])
                p.dma(o_siluz[m0:m0 + msz, c0:c0 + 512], evb[0:msz, :])
            elif m0 >= 2048 and m0 + msz <= 4352:
                p.cp(ev[0:msz, :], ps_m[0:msz, 0:512])
                p.dma(d_xbc[m0 - 2048:m0 - 2048 + msz, c0:c0 + 512], ev[0:msz, :])
            else:
                p.act(ev[0:64, :], ps_m[0:64, 0:512], AF.Identity, bias=dtbs[:, 0:1])
                p.softplus(ev2[0:64, :], ev[0:64, :], xsq[0:64, :])
                p.cp(dte_res[:, c0:c0 + 512], ev2[0:64, :])
        p.bcast(bc1[:], r_(15), 128, 512)
        for mc in range(2):
            for kc in range(8):
                p.mm(ps_m[0:128, 0:512],
                     mu_u[:, kc * 256 + mc * 128:kc * 256 + mc * 128 + 128],
                     xn2b[:, kc * 512:(kc + 1) * 512],
                     start=(kc == 0), stop=(kc == 7))
            p.tt(ev[:, 0:512], ps_m[0:128, 0:512], bc1[:], ALU.mult)
            p.act(e_f[:, mc * 512:(mc + 1) * 512], ev[:, 0:512], AF.Exp,
                  bias=negabs[:, 0:1])
            p.cp(e_b[:, mc * 512:(mc + 1) * 512], e_f[:, mc * 512:(mc + 1) * 512])
            p.dma(o_E[mc * 128:(mc + 1) * 128, c0:c0 + 512],
                  e_b[:, mc * 512:(mc + 1) * 512])
        for mc in range(2):
            p.mm(ps_r2[0:1, 0:512], onesf[:, 3:4], e_f[:, mc * 512:(mc + 1) * 512],
                 start=(mc == 0), stop=(mc == 1))
        p.cp(r_(7), ps_r2[0:1, 0:512])
        p.dma(o_scomb[0:1, c0:c0 + 512], r_(7))
        lo = max(HB, c0) - c0
        hi = min(HB + T, c0 + 512) - c0
        if hi > lo:
            for mc in range(2):
                p.red(ev[0:128, 0:1], e_f[0:128, mc * 512 + lo:mc * 512 + hi], ALU.add)
                p.tt(sdacc[:, mc:mc + 1], sdacc[:, mc:mc + 1], ev[0:128, 0:1], ALU.add)
    p.dma(o_sd[:], sdacc[:])
    p.sb_free("rows_t", "e_b", "e_f", "xn2b", "xn2f", "xn_bf", "wi")

    xw = p.sb2([128, 520], F32, "xw")
    ca = p.sb2([128, 512], F32, "ca")

    for it in range(T // 512):
        c0 = HB + it * 512
        for cc in range(18):
            p.dma(xw[:, 0:518], d_xbc[cc * 128:(cc + 1) * 128, c0 - 3:c0 + 515])
            p.ts(ca[:], xw[:, 0:512], cw[:, cc * 7:cc * 7 + 1], None, ALU.mult)
            for j in range(1, 7):
                p.stt(ca[:], xw[:, j:j + 512], cw[:, cc * 7 + j:cc * 7 + j + 1],
                      ca[:], ALU.mult, ALU.add)
            p.act(ev[:, 0:512], ca[:], AF.Silu, bias=cb[:, cc:cc + 1])
            p.cp(evb[:, 0:512], ev[:, 0:512])
            if cc < 16:
                p.dma(o_xog[cc * 128:(cc + 1) * 128, it * 512:(it + 1) * 512],
                      evb[:, 0:512])
            else:
                p.dma(o_bc[(cc - 16) * 128:(cc - 15) * 128, it * 512:(it + 1) * 512],
                      evb[:, 0:512])

    p.ts(r64a[:], dte_res[:, HB:HB + T], a64[:, 0:1], None, ALU.mult)
    for c in range(NCH):
        sl = slice(c * Q, (c + 1) * Q)
        p.scan(r64b[:, sl], r64a[:, sl], zq[:, 0:Q], 0.0, ALU.add, ALU.add)
    p.cp(le[:], r64b[:, Q - 1::Q])
    lerep = le[32:64, :].unsqueeze(2).broadcast_to([32, NCH, Q])
    v32 = r64b[32:64, :].rearrange("p (a k) -> p a k", k=Q)
    p.tt(v32, lerep, v32, ALU.subtract)
    p.tt(r64b[32:64, :], r64b[32:64, :], r64a[32:64, :], ALU.add)
    p.dma(o_lcum[:], r64b[:])
    for it in range(T // 512):
        p.act(ev[0:64, 0:512], r64b[:, it * 512:(it + 1) * 512], AF.Exp)
        p.dma(o_expl[:, it * 512:(it + 1) * 512], ev[0:64, 0:512])
    p.cp(le[0:32, :], r64b[0:32, Q - 1::Q])
    p.cp(le[32:64, :], r64b[32:64, 0::Q])
    p.dma(o_lendraw[:], le[:])
    p.act(le[:], le[:], AF.Exp)
    p.dma(o_lendexp[:], le[:])
    for it in range(T // 512):
        p.cp(evb[0:64, 0:512], dte_res[:, HB + it * 512:HB + (it + 1) * 512])
        p.dma(o_dte_bf[:, it * 512:(it + 1) * 512], evb[0:64, 0:512])

    etm = p.sb2([128, 256], BF16, "etm")
    xn2tm = p.sb2([128, 1024], BF16, "xn2tm")

    for dc in range(8):
        for tc in range(32):
            cc0 = HB + tc * Q
            p.dmat(xn2tm[:], d_xn2[:, cc0:cc0 + Q])
            p.dmat(etm[:], o_E[:, cc0:cc0 + Q])
            p.mm(ps_m[0:128, 0:256],
                 xn2tm[:, dc * 128:(dc + 1) * 128], etm[:, 0:256],
                 start=(tc == 0), stop=(tc == 31))
        p.cp(ev[:, 0:256], ps_m[0:128, 0:256])
        p.dma(o_num[dc * 128:(dc + 1) * 128, :], ev[:, 0:256])

    bsm = p.sb2([128, 64], BF16, "bsm")
    btl = p.sb2([128, 64], BF16, "btl")
    xtm = p.sb2([128, 2048], BF16, "xtm")
    lsm = p.sb2([128, 64], F32, "lsm")
    dtesm = p.sb2([128, 64], BF16, "dtesm")
    dtesmf = p.sb2([128, 64], F32, "dtesmf")
    lbr = p.sb2([128, 64], F32, "lbr")
    ffac = p.sb2([128, 64], F32, "ffac")
    sfw = p.sb2([64, 2048], F32, "sfw")
    sbw = p.sb2([64, 2048], F32, "sbw")
    ccol = p.sb2([64, 1], F32, "ccol")
    ccolb = p.sb2([64, 1], BF16, "ccolb")
    idt = p.sb2([128, 128], F32, "idt")
    p.dma(idt[:], ident)
    d_lrawT = p.dram("d_lrawT", [NCH, 64])
    tst = p.sb2([128, 128], F32, "tst")
    p.memset(tst[:], 0.0)
    p.dma(tst[0:64, 0:NCH], o_lendraw[:])
    p.transp(ps_m[0:128, 0:128], tst[:], idt[:])
    p.cp(tst[:], ps_m[0:128, 0:128])
    p.dma(d_lrawT[:], tst[0:NCH, 0:64])
    lee_all = p.sb2([64, 2048], F32, "lee_all")
    p.dma(lee_all[:], o_lendexp.flatten().unsqueeze(0).broadcast_to([64, 2048]))
    yb_sb = p.sb2([128, 32], F32, "yb_sb")
    p.memset(sfw[:], 0.0)
    p.memset(sbw[:], 0.0)
    for dirn in range(2):
        S = sfw if dirn == 0 else sbw
        order = range(NCH) if dirn == 0 else range(NCH - 1, -1, -1)
        for c in order:
            sl = slice(c * Q, (c + 1) * Q)
            p.dmat(bsm[:], o_bc[dirn * 128:dirn * 128 + 64, sl])
            p.dmat(xtm[:], o_xog[:, sl])
            p.dmat(dtesm[:], o_dte_bf[:, sl])
            p.cp(dtesmf[:], dtesm[:])
            p.memset(tst[:], 0.0)
            p.cp(tst[0:64, 0:Q], r64b[:, sl])
            p.transp(ps_m[0:128, 0:128], tst[:], idt[:])
            p.cp(lsm[:], ps_m[0:128, 0:64])
            p.dma(lbr[:], d_lrawT[c:c + 1, :].broadcast_to([128, 64]))
            p.tt(ffac[:], lbr[:], lsm[:], ALU.subtract)
            p.act(ffac[:], ffac[:], AF.Exp)
            p.tt(ffac[:], ffac[:], dtesmf[:], ALU.mult)
            for h in range(32):
                p.ts(btl[:], bsm[:], ffac[:, dirn * 32 + h:dirn * 32 + h + 1],
                     None, ALU.mult)
                p.mm(ps_big[0:64, h * 64:(h + 1) * 64], btl[:],
                     xtm[:, h * 64:(h + 1) * 64], start=True, stop=True)
            leap = lee_all[:, dirn * 1024 + c:dirn * 1024 + c + 31 * 32 + 1:32]
            leap = leap.unsqueeze(2).broadcast_to([64, 32, 64])
            p.tt(S[:].rearrange("p (a k) -> p a k", k=64),
                 S[:].rearrange("p (a k) -> p a k", k=64), leap, ALU.mult)
            p.tt(S[:], S[:], ps_big[0:64, :], ALU.add)
        p.dma(o_sfin[dirn * 64:dirn * 64 + 64, :], S[:])
        ccol_c = T - 1 if dirn == 0 else 0
        p.dma(ccolb[:], o_bc[dirn * 128 + 64:dirn * 128 + 128, ccol_c:ccol_c + 1], slow=True)
        p.cp(ccol[:], ccolb[:])
        for mc in range(16):
            p.mm(ps_m[0:128, 0:1], S[:, mc * 128:(mc + 1) * 128], ccol[:],
                 start=True, stop=True)
            p.cp(yb_sb[:, mc * 2 + dirn:mc * 2 + dirn + 1], ps_m[0:128, 0:1])
    for mc in range(16):
        p.dma(o_ybound[mc * 128:(mc + 1) * 128, 0:2], yb_sb[:, mc * 2:mc * 2 + 2])
    return p


def _core_slice(x, core):
    b, hh = core // 2, core % 2
    t0 = hh * T
    arr = np.zeros((1024, TE), np.float32)
    lo, hi = t0 - HB, t0 + T + HB
    slo, shi = max(lo, 0), min(hi, 8192)
    arr[:, (slo - lo):(slo - lo) + (shi - slo)] = x[b, slo:shi, :].T
    return arr


def host_l0a(inputs):
    x = inputs["L0_attn_u"]
    base = {
        "w_inproj": np.ascontiguousarray(inputs["L0_attn_in_proj_w"].T),
        "conv_w": np.ascontiguousarray(inputs["L0_attn_conv1d_w"][:, 0, :]),
        "conv_b": np.ascontiguousarray(inputs["L0_attn_conv1d_b"][:, None]),
        "dtb": np.ascontiguousarray(
            np.concatenate([inputs["L0_attn_dt_bias"]] * 2)[:, None]),
        "alog": np.ascontiguousarray(
            np.concatenate([inputs["L0_attn_A_log"]] * 2)[:, None]),
        "mu": np.ascontiguousarray(inputs["L0_ffn_mu"].reshape(1024, 256)),
        "lngb": np.ascontiguousarray(
            np.stack([inputs["L0_ffn_ln_gamma"], inputs["L0_ffn_ln_beta"]], 1)),
        "scale1": np.ascontiguousarray(inputs["L0_ffn_scale"].reshape(1, 1)),
        "ident": np.eye(128, dtype=np.float32),
    }
    return [dict(base, x_ext=_core_slice(x, c)) for c in range(NCORES)]


def kernel(**inputs):
    raise NotImplementedError("under construction")


def build_l0b():
    p = P("l0b")
    p.warm()
    x_in = p.inp("x", [1024, T])
    siluz = p.inp("siluz_bf", [2048, T], BF16)
    xog = p.inp("xog_bf", [2048, T], BF16)
    bc_bf = p.inp("bc_bf", [256, T], BF16)
    lcum = p.inp("lcum", [64, T])
    expl = p.inp("expl", [64, T])
    lendraw = p.inp("lendraw", [64, NCH])
    lendexp = p.inp("lendexp", [64, NCH])
    dte_bf = p.inp("dte_bf", [64, T], BF16)
    e_in = p.inp("ffn_E_bf", [256, T], BF16)
    scomb = p.inp("ffn_scomb", [1, T])
    num_a = p.inp("num_a", [1024, 256])
    num_b = p.inp("num_b", [1024, 256])
    sd_a = p.inp("sd_a", [128, 2])
    sd_b = p.inp("sd_b", [128, 2])
    sinit = p.inp("sinit", [128, 2048])
    ybin = p.inp("ybin", [2048, 2])
    dwT = p.inp("dwT", [2048, 32])
    db = p.inp("db", [32, 1])
    rmsw = p.inp("rmsw", [128, 16])
    outw = p.inp("outw", [2048, 1024])
    w1s = p.inp("w1s", [1024, 4096])
    b1s = p.inp("b1s", [16, 256])
    w2s = p.inp("w2s", [256, 16384])
    b2s = p.inp("b2s", [256, 1024])
    maskf = p.inp("maskf", [128, 128])
    maskb = p.inp("maskb", [128, 128])
    ident = p.inp("ident", [128, 128])

    o_xout = p.out("xout", [1024, T])
    o_g = p.out("g_out", [1024, T])

    d_y0 = p.dram("d_y0", [2048, T])
    d_y1 = p.dram("d_y1", [2048, T])
    d_so = p.dram("d_so", [256, 1024], BF16)
    d_D = p.dram("d_D", [32, T])
    d_lrawT = p.dram("d_lrawT", [NCH, 64])

    ps_m = p.psum([128, 512], F32, "ps_m")
    ps_big = p.psum([128, 2048], F32, "ps_big")
    ps_y = p.psum([64, 128], F32, "ps_y")
    ps_g = p.psum([128, 128], F32, "ps_g")

    ev = p.sb2([128, 512], F32, "ev")
    ev2 = p.sb2([128, 512], F32, "ev2")
    evb = p.sb2([128, 512], BF16, "evb")
    idt = p.sb2([128, 128], F32, "idt")
    p.dma(idt[:], ident)

    sin_bf = p.sb2([128, 8 * 256], BF16, "sin_bf")
    w1sb = p.sb2([128, 8 * 4096], BF16, "w1sb")
    sdr = p.sb2([128, 2], F32, "sdr")
    bcr = p.sb2([128, 512], F32, "bcr")
    row1 = p.sb2([1, 512], F32, "row1")
    b1t = p.sb2([16, 256], F32, "b1t")
    het = p.sb2([16, 256], F32, "het")
    hebf = p.sb2([16, 256], BF16, "hebf")
    heT = p.sb2([128, 32], BF16, "heT")
    w2t = p.sb2([128, 1024], BF16, "w2t")
    sot = p.sb2([16, 1024], F32, "sot")
    sotb = p.sb2([16, 1024], BF16, "sotb")
    d_sdr = p.dram("d_sdr", [256, 1], F32)
    p.dma(sdr[:, 0:2], sd_a)
    p.dma(ev[0:128, 0:2], sd_b)
    p.tt(sdr[:, 0:2], sdr[:, 0:2], ev[0:128, 0:2], ALU.add)
    p.recip(sdr[:, 0:2], sdr[:, 0:2])
    for mc in range(2):
        p.dma(d_sdr[mc * 128:(mc + 1) * 128, 0:1], sdr[:, mc:mc + 1])
    p.dma(row1[0:1, 0:256], d_sdr.rearrange("a b -> b a"))
    p.bcast(bcr[:, 0:256], row1[0:1, 0:256], 128, 256)
    for kc in range(8):
        p.cast_dma(w1sb[:, kc * 4096:(kc + 1) * 4096],
                   w1s[kc * 128:(kc + 1) * 128, :])
        p.dma(ev[:, 0:256], num_a[kc * 128:(kc + 1) * 128, :])
        p.dma(ev2[:, 0:256], num_b[kc * 128:(kc + 1) * 128, :])
        p.tt(ev[:, 0:256], ev[:, 0:256], ev2[:, 0:256], ALU.add)
        p.tt(ev[:, 0:256], ev[:, 0:256], bcr[:, 0:256], ALU.mult)
        p.cp(sin_bf[:, kc * 256:(kc + 1) * 256], ev[:, 0:256])
    for e in range(16):
        for kc in range(8):
            p.mm(ps_m[0:16, 0:256],
                 sin_bf[:, kc * 256 + e * 16:kc * 256 + (e + 1) * 16 if False else kc * 256 + e * 16 + 16],
                 w1sb[:, kc * 4096 + e * 256:kc * 4096 + (e + 1) * 256],
                 start=(kc == 0), stop=(kc == 7))
        p.dma(b1t[:], b1s[e:e + 1, :].broadcast_to([16, 256]))
        p.tt(het[:], ps_m[0:16, 0:256], b1t[:], ALU.add)
        p.act(het[:], het[:], AF.Gelu)
        p.cp(hebf[:], het[:])
        p.dmat(heT[:, 0:16], hebf[0:16, 0:128])
        p.dmat(heT[:, 16:32], hebf[0:16, 128:256])
        for half in range(2):
            p.cast_dma(w2t[:], w2s[half * 128:(half + 1) * 128,
                                   e * 1024:(e + 1) * 1024])
            for n0 in (0, 512):
                p.mm(ps_big[0:16, n0:n0 + 512], heT[:, half * 16:(half + 1) * 16],
                     w2t[:, n0:n0 + 512], start=(half == 0), stop=(half == 1))
        p.dma(b1t[:] if False else sot[:], b2s[e * 16:(e + 1) * 16, :])
        p.tt(sot[:], sot[:], ps_big[0:16, 0:1024], ALU.add)
        p.cp(sotb[:], sot[:])
        p.dma(d_so[e * 16:(e + 1) * 16, :], sotb[:])
    p.sb_free("sotb", "sot", "w2t", "heT", "hebf", "het", "b1t", "row1",
              "bcr", "sdr", "w1sb", "sin_bf")
    so_sb = p.sb2([128, 2 * 1024], BF16, "so_sb")
    e2b = p.sb2([128, 2 * 512], BF16, "e2b")
    bcc = p.sb2([128, 512], F32, "bcc")
    rowc = p.sb2([1, 512], F32, "rowc")
    for esc in range(2):
        p.cast_dma(so_sb[:, esc * 1024:(esc + 1) * 1024],
                   d_so[esc * 128:(esc + 1) * 128, :])
    for it in range(T // 512):
        c0 = it * 512
        p.dma(rowc[0:1, :], scomb[0:1, c0:c0 + 512])
        p.recip(rowc[0:1, :], rowc[0:1, :])
        p.bcast(bcc[:], rowc[0:1, :], 128, 512)
        for esc in range(2):
            p.dma(evb[:, 0:512], e_in[esc * 128:(esc + 1) * 128, c0:c0 + 512])
            p.cp(ev[:, 0:512], evb[:, 0:512])
            p.tt(ev[:, 0:512], ev[:, 0:512], bcc[:], ALU.mult)
            p.cp(e2b[:, esc * 512:(esc + 1) * 512], ev[:, 0:512])
        for oc in range(8):
            for esc in range(2):
                p.mm(ps_m[0:128, 0:512],
                     so_sb[:, esc * 1024 + oc * 128:esc * 1024 + (oc + 1) * 128],
                     e2b[:, esc * 512:(esc + 1) * 512],
                     start=(esc == 0), stop=(esc == 1))
            p.act(ev[:, 0:512], ps_m[0:128, 0:512], AF.Silu)
            p.dma(o_g[oc * 128:(oc + 1) * 128, c0:c0 + 512], ev[:, 0:512])
    p.sb_free("rowc", "bcc", "e2b", "so_sb")

    r64b = p.sb2([64, T], F32, "r64b")
    p.dma(r64b[:], lcum)
    tst = p.sb2([128, 128], F32, "tst")
    p.memset(tst[:], 0.0)
    p.dma(tst[0:64, 0:NCH], lendraw)
    p.transp(ps_g[0:128, 0:128], tst[:], idt[:])
    p.cp(tst[:], ps_g[0:128, 0:128])
    p.dma(d_lrawT[:], tst[0:NCH, 0:64])
    lee_all = p.sb2([64, 2048], F32, "lee_all")
    p.dma(lee_all[:], lendexp.flatten().unsqueeze(0).broadcast_to([64, 2048]))
    mskt = p.sb2([128, 2 * 128], F32, "mskt")
    p.dma(mskt[:, 0:128], maskf)
    p.dma(mskt[:, 128:256], maskb)
    bcl = p.sb2([128, 4096], F32, "bcl")
    msbuf = p.sb2([128, 4096], F32, "msbuf")
    msbf = p.sb2([128, 4096], BF16, "msbf")
    ebc = p.sb2([64, 4096], F32, "ebc")
    gsb = p.sb2([128, 128], F32, "gsb")
    bsb = p.sb2([64, 128], BF16, "bsb")
    csb = p.sb2([64, 128], BF16, "csb")
    xtm = p.sb2([128, 2048], BF16, "xtm")
    lsm = p.sb2([128, 64], F32, "lsm")
    dtesm = p.sb2([128, 64], BF16, "dtesm")
    dtesmf = p.sb2([128, 64], F32, "dtesmf")
    lbr = p.sb2([128, 64], F32, "lbr")
    ffac = p.sb2([128, 64], F32, "ffac")
    bsm = p.sb2([128, 64], BF16, "bsm")
    btl = p.sb2([128, 64], BF16, "btl")
    scur = p.sb2([64, 2048], F32, "scur")
    s_bf = p.sb2([64, 2048], BF16, "s_bf")
    ysb = p.sb2([64, 128], F32, "ysb")
    for dirn in range(2):
        p.dma(scur[:], sinit[dirn * 64:dirn * 64 + 64, :])
        order = range(NCH) if dirn == 0 else range(NCH - 1, -1, -1)
        d_y = d_y0 if dirn == 0 else d_y1
        for c in order:
            sl = slice(c * Q, (c + 1) * Q)
            p.dma(bcl[:].rearrange("p (a k) -> p a k", k=Q),
                  lcum[dirn * 32:dirn * 32 + 32, sl].unsqueeze(0)
                  .broadcast_to([128, 32, Q]))
            p.memset(tst[:], 0.0)
            p.cp(tst[0:64, 0:Q], r64b[:, sl])
            p.transp(ps_g[0:128, 0:128], tst[:], idt[:])
            p.cp(lsm[:], ps_g[0:128, 0:64])
            lsr = lsm[:, dirn * 32:dirn * 32 + 32].unsqueeze(2).broadcast_to([128, 32, Q])
            mview = msbuf[:].rearrange("p (a k) -> p a k", k=Q)
            p.tt(mview, bcl[:].rearrange("p (a k) -> p a k", k=Q), lsr, ALU.subtract)
            mrep = mskt[:, dirn * 128:dirn * 128 + 128].unsqueeze(1).broadcast_to([128, 32, Q])
            p.tt(mview, mview, mrep, ALU.add)
            p.act(msbuf[:], msbuf[:], AF.Exp)
            p.dma(bsb[:], bc_bf[dirn * 128:dirn * 128 + 64, sl])
            p.dma(csb[:], bc_bf[dirn * 128 + 64:dirn * 128 + 128, sl])
            p.mm(ps_g[0:128, 0:128], bsb[:], csb[:], start=True, stop=True)
            p.cp(gsb[:], ps_g[0:128, 0:128])
            grep = gsb[:].unsqueeze(1).broadcast_to([128, 32, Q])
            p.tt(mview, mview, grep, ALU.mult)
            p.dmat(dtesm[:], dte_bf[:, sl])
            p.cp(dtesmf[:], dtesm[:])
            drep = dtesmf[:, dirn * 32:dirn * 32 + 32].unsqueeze(2).broadcast_to([128, 32, Q])
            p.tt(mview, mview, drep, ALU.mult)
            p.cp(msbf[:], msbuf[:])
            p.dmat(xtm[:], xog[:, sl])
            p.cp(s_bf[:], scur[:])
            p.dma(ebc[:].rearrange("p (a k) -> p a k", k=Q),
                  expl[dirn * 32:dirn * 32 + 32, sl].unsqueeze(0)
                  .broadcast_to([64, 32, Q]))
            for h in range(32):
                p.mm(ps_y[0:64, 0:Q], xtm[:, h * 64:(h + 1) * 64],
                     msbf[:, h * Q:(h + 1) * Q], start=True, stop=False)
                p.mm(ps_y[0:64, 0:Q], s_bf[:, h * 64:(h + 1) * 64],
                     csb[:], start=False, stop=True)
                p.tt(ysb[:], ps_y[0:64, 0:Q], ebc[:, h * Q:(h + 1) * Q], ALU.mult)
                p.dma(d_y[h * 64:(h + 1) * 64, sl], ysb[:])
            p.dma(lbr[:], d_lrawT[c:c + 1, :].broadcast_to([128, 64]))
            p.tt(ffac[:], lbr[:], lsm[:], ALU.subtract)
            p.act(ffac[:], ffac[:], AF.Exp)
            p.tt(ffac[:], ffac[:], dtesmf[:], ALU.mult)
            p.dmat(bsm[:], bc_bf[dirn * 128:dirn * 128 + 64, sl])
            for h in range(32):
                p.ts(btl[:], bsm[:], ffac[:, dirn * 32 + h:dirn * 32 + h + 1],
                     None, ALU.mult)
                p.mm(ps_big[0:64, h * 64:(h + 1) * 64], btl[:],
                     xtm[:, h * 64:(h + 1) * 64], start=True, stop=True)
            leap = lee_all[:, dirn * 1024 + c:dirn * 1024 + c + 31 * 32 + 1:32]
            leap = leap.unsqueeze(2).broadcast_to([64, 32, 64])
            p.tt(scur[:].rearrange("p (a k) -> p a k", k=64),
                 scur[:].rearrange("p (a k) -> p a k", k=64), leap, ALU.mult)
            p.tt(scur[:], scur[:], ps_big[0:64, :], ALU.add)
    p.sb_free("ysb", "s_bf", "scur", "btl", "bsm", "ffac", "lbr", "dtesmf",
              "dtesm", "lsm", "xtm", "csb", "bsb", "gsb", "ebc", "msbf",
              "msbuf", "bcl", "mskt", "lee_all", "tst", "r64b")

    dw_sb = p.sb2([128, 16 * 32], BF16, "dw_sb")
    dbs = p.sb2([32, 1], F32, "dbs")
    xgb = p.sb2([128, 512], BF16, "xgb")
    for kc in range(16):
        p.cast_dma(dw_sb[:, kc * 32:(kc + 1) * 32], dwT[kc * 128:(kc + 1) * 128, :])
    p.dma(dbs[:], db)
    for it in range(T // 512):
        c0 = it * 512
        for kc in range(16):
            p.dma(xgb[:], xog[kc * 128:(kc + 1) * 128, c0:c0 + 512])
            p.mm(ps_m[0:32, 0:512], dw_sb[:, kc * 32:(kc + 1) * 32], xgb[:],
                 start=(kc == 0), stop=(kc == 15))
        p.act(ev[0:32, 0:512], ps_m[0:32, 0:512], AF.Identity, bias=dbs[:, 0:1])
        p.dma(d_D[:, c0:c0 + 512], ev[0:32, 0:512])
    ytile = p.sb2([128, 16 * 512], F32, "ytile")
    unb = p.sb2([128, 16 * 512], BF16, "unb")
    outw_sb = p.sb2([128, 16 * 1024], BF16, "outw_sb")
    rmsw_sb = p.sb2([128, 16], F32, "rmsw_sb")
    dbc = p.sb2([128, 512], F32, "dbc")
    ybt = p.sb2([128, 2], F32, "ybt")
    rowr = p.sb2([1, 512], F32, "rowr")
    bcy = p.sb2([128, 512], F32, "bcy")
    for kc in range(16):
        p.cast_dma(outw_sb[:, kc * 1024:(kc + 1) * 1024],
                   outw[kc * 128:(kc + 1) * 128, :])
    p.dma(rmsw_sb[:], rmsw)
    for it in range(T // 512):
        c0 = it * 512
        for kc in range(16):
            yt = ytile[:, kc * 512:(kc + 1) * 512]
            if it == 0:
                p.dma(yt[:, 1:512], d_y0[kc * 128:(kc + 1) * 128, 0:511])
                p.dma(ybt[:], ybin[kc * 128:(kc + 1) * 128, 0:2])
                p.cp(yt[:, 0:1], ybt[:, 0:1])
            else:
                p.dma(yt[:, 0:512], d_y0[kc * 128:(kc + 1) * 128, c0 - 1:c0 + 511])
            if it == T // 512 - 1:
                p.dma(ev[:, 0:511], d_y1[kc * 128:(kc + 1) * 128, c0 + 1:c0 + 512])
                p.dma(ybt[:], ybin[kc * 128:(kc + 1) * 128, 0:2])
                p.cp(ev[:, 511:512], ybt[:, 1:2])
            else:
                p.dma(ev[:, 0:512], d_y1[kc * 128:(kc + 1) * 128, c0 + 1:c0 + 513])
            p.tt(yt, yt, ev[:, 0:512], ALU.add)
            p.dma(dbc[0:64, :],
                  d_D[2 * kc:2 * kc + 1, c0:c0 + 512].broadcast_to([64, 512]))
            p.dma(dbc[64:128, :],
                  d_D[2 * kc + 1:2 * kc + 2, c0:c0 + 512].broadcast_to([64, 512]))
            p.dma(xgb[:], xog[kc * 128:(kc + 1) * 128, c0:c0 + 512])
            p.cp(ev[:, 0:512], xgb[:])
            p.tt(ev[:, 0:512], ev[:, 0:512], dbc[:], ALU.mult)
            p.tt(yt, yt, ev[:, 0:512], ALU.add)
            p.act(ev[:, 0:512], yt, AF.Square)
            p.mm(ps_m[0:1, 0:512], _ones_row(p)[:, 0:1], ev[:, 0:512],
                 start=(kc == 0), stop=(kc == 15))
        p.cp(rowr[0:1, :], ps_m[0:1, 0:512])
        p.ts(rowr[0:1, :], rowr[0:1, :], 1.0 / 2048.0, 1e-5, ALU.mult, ALU.add)
        p.act(rowr[0:1, :], rowr[0:1, :], AF.Sqrt)
        p.recip(rowr[0:1, :], rowr[0:1, :])
        p.bcast(bcy[:], rowr[0:1, :], 128, 512)
        for kc in range(16):
            yt = ytile[:, kc * 512:(kc + 1) * 512]
            p.tt(yt, yt, bcy[:], ALU.mult)
            p.ts(yt, yt, rmsw_sb[:, kc:kc + 1], None, ALU.mult)
            p.dma(evb[:, 0:512], siluz[kc * 128:(kc + 1) * 128, c0:c0 + 512])
            p.cp(ev[:, 0:512], evb[:, 0:512])
            p.tt(yt, yt, ev[:, 0:512], ALU.mult)
            p.cp(unb[:, kc * 512:(kc + 1) * 512], yt)
        for oc in range(8):
            for kc in range(16):
                p.mm(ps_m[0:128, 0:512],
                     outw_sb[:, kc * 1024 + oc * 128:kc * 1024 + (oc + 1) * 128],
                     unb[:, kc * 512:(kc + 1) * 512],
                     start=(kc == 0), stop=(kc == 15))
            p.dma(ev[:, 0:512], o_g[oc * 128:(oc + 1) * 128, c0:c0 + 512])
            p.tt(ev[:, 0:512], ev[:, 0:512], ps_m[0:128, 0:512], ALU.mult)
            p.dma(ev2[:, 0:512], x_in[oc * 128:(oc + 1) * 128, c0:c0 + 512])
            p.tt(ev[:, 0:512], ev[:, 0:512], ev2[:, 0:512], ALU.add)
            p.dma(o_xout[oc * 128:(oc + 1) * 128, c0:c0 + 512], ev[:, 0:512])
    return p


def _ones_row(p):
    if not hasattr(p, "_onesr"):
        p._onesr = p.sb2([128, 4], F32, "onesr")
        p.memset(p._onesr[:], 1.0)
    return p._onesr


def host_l0b(inputs, a0):
    x = inputs["L0_attn_u"]
    Z = np.zeros
    w1 = inputs["L0_ffn_w1"]
    w2 = inputs["L0_ffn_w2"]
    base = {
        "dwT": np.ascontiguousarray(inputs["L0_attn_D_w"].T),
        "db": np.ascontiguousarray(inputs["L0_attn_D_b"][:, None]),
        "rmsw": np.ascontiguousarray(inputs["L0_attn_rms_w"].reshape(16, 128).T),
        "outw": np.ascontiguousarray(inputs["L0_attn_outproj_w"].T),
        "w1s": np.ascontiguousarray(w1.transpose(1, 0, 2).reshape(1024, 4096)),
        "b1s": np.ascontiguousarray(inputs["L0_ffn_b1"]),
        "w2s": np.ascontiguousarray(w2.transpose(1, 0, 2).reshape(256, 16384)),
        "b2s": np.ascontiguousarray(np.repeat(inputs["L0_ffn_b2"], 16, axis=0)),
        "maskf": np.where(np.arange(128)[:, None] <= np.arange(128)[None, :],
                          0.0, -1e9).astype(np.float32),
        "maskb": np.where(np.arange(128)[:, None] >= np.arange(128)[None, :],
                          0.0, -1e9).astype(np.float32),
        "ident": np.eye(128, dtype=np.float32),
    }
    ims = []
    for c in range(NCORES):
        b, hh = c // 2, c % 2
        par = a0[c ^ 1]
        me = a0[c]
        sinit = np.zeros((128, 2048), np.float32)
        ybin = np.zeros((2048, 2), np.float32)
        if hh == 1:
            sinit[0:64] = par["sfin"][0:64]
            ybin[:, 0] = par["ybound"][:, 0]
        else:
            sinit[64:128] = par["sfin"][64:128]
            ybin[:, 1] = par["ybound"][:, 1]
        im = dict(base,
                  x=np.ascontiguousarray(x[b, hh * T:(hh + 1) * T, :].T),
                  siluz_bf=np.ascontiguousarray(me["siluz_bf"][:, HB:HB + T]),
                  xog_bf=me["xog_bf"], bc_bf=me["bc_bf"], lcum=me["lcum"],
                  expl=me["expl"], lendraw=me["lendraw"], lendexp=me["lendexp"],
                  dte_bf=me["dte_bf"],
                  ffn_E_bf=np.ascontiguousarray(me["ffn_E_bf"][:, HB:HB + T]),
                  ffn_scomb=np.ascontiguousarray(me["ffn_scomb"][:, HB:HB + T]),
                  num_a=me["ffn_num"], num_b=par["ffn_num"],
                  sd_a=me["ffn_sd"], sd_b=par["ffn_sd"],
                  sinit=sinit, ybin=ybin)
        ims.append(im)
    return ims


def build_l1a():
    p = P("l1a")
    p.warm()
    x_ext = p.inp("x_ext", [1024, TE])
    wgT = p.inp("wgT", [1024, 1536])
    winT = p.inp("winT", [1024, 1536])
    cw4 = p.inp("cw4", [1536, 4])
    cb4 = p.inp("cb4", [1536, 1])
    rginT = p.inp("rginT", [1536, 1536])
    rginb = p.inp("rginb", [128, 12])
    rggT = p.inp("rggT", [1536, 1536])
    rggb = p.inp("rggb", [128, 12])
    lamr = p.inp("lamr", [128, 12])
    n1w = p.inp("n1w", [128, 8])
    lgbit = p.inp("lgbit", [128, 16])

    o_xn1 = p.out("xn1_bf", [1024, T], BF16)
    o_gg = p.out("gategelu_bf", [1536, T], BF16)
    o_hrec = p.out("hrec", [1536, T])
    o_cumP = p.out("cumP", [1536, T])
    o_hfin = p.out("hfin", [128, 12])
    o_xbit = p.out("xbit_bf", [1024, T], BF16)
    o_locmax = p.out("locmax", [1, 1])

    d_uc = p.dram("d_uc", [1536, TE], BF16)
    d_ucv = p.dram("d_ucv", [1536, T], BF16)
    d_al = p.dram("d_al", [1536, T])
    d_xb = p.dram("d_xb", [1536, T])

    ps_m = p.psum([128, 512], F32, "ps_m")
    ps_g2 = p.psum([128, 512], F32, "ps_g2")
    ps_r1 = p.psum([128, 512], F32, "ps_r1")
    ps_r2 = p.psum([128, 512], F32, "ps_r2")

    onesf = p.sb2([128, 8], F32, "onesf")
    p.memset(onesf[:], 1.0)
    xs = p.sb2([128, 512], F32, "xs")
    xsq = p.sb2([128, 512], F32, "xsq")
    bc1 = p.sb2([128, 512], F32, "bc1")
    bc2 = p.sb2([128, 512], F32, "bc2")
    bc3 = p.sb2([128, 512], F32, "bc3")
    ev = p.sb2([128, 512], F32, "ev")
    ev2 = p.sb2([128, 512], F32, "ev2")
    evb = p.sb2([128, 512], BF16, "evb")
    rows_t = p.sb2([1, 8 * 512], F32, "rows_t")
    n1s = p.sb2([128, 8], F32, "n1s")
    lgbs = p.sb2([128, 16], F32, "lgbs")
    mslam = p.sb2([128, 12], F32, "mslam")
    mxa = p.sb2([128, 1], F32, "mxa")
    rgibs = p.sb2([128, 12], F32, "rgibs")
    rggbs = p.sb2([128, 12], F32, "rggbs")
    p.dma(n1s[:], n1w)
    p.dma(lgbs[:], lgbit)
    p.dma(rgibs[:], rginb)
    p.dma(rggbs[:], rggb)
    lam2 = p.sb2([128, 12], F32, "lam2")
    rowm = p.sb2([1, 128], F32, "rowm")
    p.dma(lam2[:], lamr)
    p.softplus(mslam[:], lam2[:], xsq[:, 0:12])
    p.ts(mslam[:], mslam[:], -8.0)
    p.memset(mxa[:], 0.0)

    def r_(i):
        return rows_t[0:1, i * 512:(i + 1) * 512]

    wpu = p.sb2([128, 8 * 1536], BF16, "wpu")
    wpg = p.sb2([128, 8 * 1536], BF16, "wpg")
    hbf = p.sb2([128, 8 * 512], BF16, "hbf")
    for kc in range(8):
        p.cast_dma(wpu[:, kc * 1536:(kc + 1) * 1536],
                   winT[kc * 128:(kc + 1) * 128, :])
        p.cast_dma(wpg[:, kc * 1536:(kc + 1) * 1536],
                   wgT[kc * 128:(kc + 1) * 128, :])
    for it in range(TE // 512):
        c0 = it * 512
        for kc in range(8):
            p.dma(xs[:], x_ext[kc * 128:(kc + 1) * 128, c0:c0 + 512])
            p.act(xsq[:], xs[:], AF.Square)
            p.mm(ps_r1[0:1, 0:512], onesf[:, 0:1], xs[:],
                 start=(kc == 0), stop=(kc == 7))
            p.mm(ps_r2[0:1, 0:512], onesf[:, 1:2], xsq[:],
                 start=(kc == 0), stop=(kc == 7))
        p.cp(r_(0), ps_r1[0:1, 0:512])
        p.cp(r_(1), ps_r2[0:1, 0:512])
        p.ts(r_(2), r_(0), 1.0 / D)
        p.act(r_(4), r_(2), AF.Square)
        p.ts(r_(5), r_(1), 1.0 / D)
        p.tt(r_(5), r_(5), r_(4), ALU.subtract)
        p.ts(r_(6), r_(5), 1e-5, None, ALU.add)
        p.act(r_(7), r_(6), AF.Sqrt)
        p.recip(r_(3), r_(7))
        p.recip(r_(4), r_(6))
        p.tt(r_(4), r_(5), r_(4), ALU.mult)
        p.ts(r_(5), r_(4), 1e-5, None, ALU.add)
        p.act(r_(5), r_(5), AF.Sqrt)
        p.recip(r_(5), r_(5))
        p.tt(r_(5), r_(3), r_(5), ALU.mult)
        p.ts(r_(6), r_(4), 1e-3, None, ALU.add)
        p.act(r_(6), r_(6), AF.Sqrt)
        p.recip(r_(6), r_(6))
        p.tt(r_(6), r_(3), r_(6), ALU.mult)
        p.bcast(bc1[:], r_(2), 128, 512)
        p.bcast(bc2[:], r_(3), 128, 512)
        p.bcast(bc3[:], r_(5), 128, 512)
        incore = c0 + 512 > HB and c0 < HB + T
        lo = max(HB, c0) - c0
        hi = min(HB + T, c0 + 512) - c0
        for kc in range(8):
            p.dma(xs[:], x_ext[kc * 128:(kc + 1) * 128, c0:c0 + 512])
            p.tt(xsq[:], xs[:], bc1[:], ALU.subtract)
            if incore:
                p.tt(ev[:], xsq[:], bc2[:], ALU.mult)
                p.cp(evb[:], ev[:])
                p.dma(o_xn1[kc * 128:(kc + 1) * 128, c0 - HB + lo:c0 - HB + hi],
                      evb[:, lo:hi])
            p.tt(xsq[:], xsq[:], bc3[:], ALU.mult)
            p.ts(xsq[:], xsq[:], n1s[:, kc:kc + 1], None, ALU.mult)
            p.cp(hbf[:, kc * 512:(kc + 1) * 512], xsq[:])
        if incore:
            p.bcast(bc3[:], r_(6), 128, 512)
            for kc in range(8):
                p.dma(xs[:], x_ext[kc * 128:(kc + 1) * 128, c0:c0 + 512])
                p.tt(xsq[:], xs[:], bc1[:], ALU.subtract)
                p.tt(xsq[:], xsq[:], bc3[:], ALU.mult)
                p.ts(xsq[:], xsq[:], lgbs[:, 2 * kc:2 * kc + 1],
                     lgbs[:, 2 * kc + 1:2 * kc + 2], ALU.mult, ALU.add)
                p.cp(evb[:], xsq[:])
                p.dma(o_xbit[kc * 128:(kc + 1) * 128, c0 - HB + lo:c0 - HB + hi],
                      evb[:, lo:hi])
                p.red(ev[:, 0:1], xsq[:, lo:hi], ALU.max, absval=True)
                p.tt(mxa[:], mxa[:], ev[:, 0:1], ALU.max)
        for mc in range(12):
            for kc in range(8):
                p.mm(ps_m[0:128, 0:512],
                     wpu[:, kc * 1536 + mc * 128:kc * 1536 + (mc + 1) * 128],
                     hbf[:, kc * 512:(kc + 1) * 512],
                     start=(kc == 0), stop=(kc == 7))
            p.cp(evb[:], ps_m[0:128, 0:512])
            p.dma(d_uc[mc * 128:(mc + 1) * 128, c0:c0 + 512], evb[:])
            if incore:
                for kc in range(8):
                    p.mm(ps_g2[0:128, 0:512],
                         wpg[:, kc * 1536 + mc * 128:kc * 1536 + (mc + 1) * 128],
                         hbf[:, kc * 512:(kc + 1) * 512],
                         start=(kc == 0), stop=(kc == 7))
                p.act(ev[:], ps_g2[0:128, 0:512], AF.Gelu)
                p.cp(evb[:], ev[:])
                p.dma(o_gg[mc * 128:(mc + 1) * 128, c0 - HB + lo:c0 - HB + hi],
                      evb[:, lo:hi])
    d_mx = p.dram("d_mx", [128, 1])
    p.dma(d_mx[:], mxa[:])
    p.dma(rowm[:], d_mx.rearrange("a b -> b a"))
    p.red(rowm[0:1, 0:1], rowm[0:1, 0:128], ALU.max)
    p.dma(o_locmax[:], rowm[0:1, 0:1])
    p.sb_free("hbf", "wpg", "wpu")

    cws = p.sb2([128, 12 * 4], F32, "cws")
    cbs = p.sb2([128, 12], F32, "cbs")
    xwb = p.sb2([128, 516], BF16, "xwb")
    xw = p.sb2([128, 516], F32, "xw")
    ca = p.sb2([128, 512], F32, "ca")
    for gc in range(12):
        p.dma(cws[:, gc * 4:(gc + 1) * 4], cw4[gc * 128:(gc + 1) * 128, :])
        p.dma(cbs[:, gc:gc + 1], cb4[gc * 128:(gc + 1) * 128, :])
    for it in range(T // 512):
        c0 = HB + it * 512
        for gc in range(12):
            p.dma(xwb[:, 0:515], d_uc[gc * 128:(gc + 1) * 128, c0 - 3:c0 + 512])
            p.cp(xw[:, 0:515], xwb[:, 0:515])
            p.ts(ca[:], xw[:, 0:512], cws[:, gc * 4:gc * 4 + 1], None, ALU.mult)
            for j in range(1, 4):
                p.stt(ca[:], xw[:, j:j + 512], cws[:, gc * 4 + j:gc * 4 + j + 1],
                      ca[:], ALU.mult, ALU.add)
            p.ts(ca[:], ca[:], cbs[:, gc:gc + 1], None, ALU.add)
            p.cp(evb[:], ca[:])
            p.dma(d_ucv[gc * 128:(gc + 1) * 128, it * 512:(it + 1) * 512], evb[:])
    p.sb_free("ca", "xw", "xwb", "cbs", "cws")

    wrg1 = p.sb2([128, 12 * 1536], BF16, "wrg1")
    wrg2 = p.sb2([128, 12 * 1536], BF16, "wrg2")
    ucb = p.sb2([128, 12 * 512], BF16, "ucb")
    for kc in range(12):
        p.cast_dma(wrg1[:, kc * 1536:(kc + 1) * 1536],
                   rginT[kc * 128:(kc + 1) * 128, :])
        p.cast_dma(wrg2[:, kc * 1536:(kc + 1) * 1536],
                   rggT[kc * 128:(kc + 1) * 128, :])
    for it in range(T // 512):
        c0 = it * 512
        for kc in range(12):
            p.dma(ucb[:, kc * 512:(kc + 1) * 512],
                  d_ucv[kc * 128:(kc + 1) * 128, c0:c0 + 512])
        for gc in range(12):
            for kc in range(12):
                p.mm(ps_m[0:128, 0:512],
                     wrg2[:, kc * 1536 + gc * 128:kc * 1536 + (gc + 1) * 128],
                     ucb[:, kc * 512:(kc + 1) * 512],
                     start=(kc == 0), stop=(kc == 11))
            p.act(ev[:], ps_m[0:128, 0:512], AF.Sigmoid, bias=rggbs[:, gc:gc + 1])
            p.ts(ev[:], ev[:], mslam[:, gc:gc + 1], None, ALU.mult)
            p.act(ev[:], ev[:], AF.Exp)
            p.dma(d_al[gc * 128:(gc + 1) * 128, c0:c0 + 512], ev[:])
            p.act(ev2[:], ev[:], AF.Square)
            p.ts(ev2[:], ev2[:], -1.0, 1.000001, ALU.mult, ALU.add)
            p.act(ev2[:], ev2[:], AF.Sqrt)
            for kc in range(12):
                p.mm(ps_g2[0:128, 0:512],
                     wrg1[:, kc * 1536 + gc * 128:kc * 1536 + (gc + 1) * 128],
                     ucb[:, kc * 512:(kc + 1) * 512],
                     start=(kc == 0), stop=(kc == 11))
            p.act(ev[:], ps_g2[0:128, 0:512], AF.Sigmoid, bias=rgibs[:, gc:gc + 1])
            p.tt(ev[:], ev[:], ev2[:], ALU.mult)
            p.cp(ev2[:], ucb[:, gc * 512:(gc + 1) * 512])
            p.tt(ev[:], ev[:], ev2[:], ALU.mult)
            p.dma(d_xb[gc * 128:(gc + 1) * 128, c0:c0 + 512], ev[:])
    p.sb_free("ucb", "wrg2", "wrg1")

    sa = p.sb2([128, T], F32, "sa")
    sbx = p.sb2([128, T], F32, "sbx")
    sh = p.sb2([128, T], F32, "sh")
    hfn = p.sb2([128, 12], F32, "hfn")
    for gc in range(12):
        p.dma(sa[:], d_al[gc * 128:(gc + 1) * 128, :])
        p.dma(sbx[:], d_xb[gc * 128:(gc + 1) * 128, :])
        p.scan(sh[:], sa[:], sbx[:], 0.0, ALU.mult, ALU.add)
        p.dma(o_hrec[gc * 128:(gc + 1) * 128, :], sh[:])
        p.cp(hfn[:, gc:gc + 1], sh[:, T - 1:T])
        p.scan(sh[:], sa[:], sa[:], 1.0, ALU.mult, ALU.bypass)
        p.dma(o_cumP[gc * 128:(gc + 1) * 128, :], sh[:])
    p.dma(o_hfin[:], hfn[:])
    return p


def host_l1_xext(xout_list, core):
    b, hh = core // 2, core % 2
    arr = np.zeros((1024, TE), np.float32)
    arr[:, HB:HB + T] = xout_list[core]
    par = xout_list[core ^ 1]
    if hh == 0:
        arr[:, HB + T:HB + T + 3] = par[:, 0:3]
    else:
        arr[:, 0:HB] = par[:, T - 3:T]
    return arr


def host_l1a(inputs, xout_list):
    base = {
        "wgT": np.ascontiguousarray(inputs["L1_attn_W_gate"].T),
        "winT": np.ascontiguousarray(inputs["L1_attn_W_in"].T),
        "cw4": np.ascontiguousarray(inputs["L1_attn_conv_w"][:, 0, :]),
        "cb4": np.ascontiguousarray(inputs["L1_attn_conv_b"][:, None]),
        "rginT": np.ascontiguousarray(inputs["L1_attn_rg_in_w"].T),
        "rginb": np.ascontiguousarray(inputs["L1_attn_rg_in_b"].reshape(12, 128).T),
        "rggT": np.ascontiguousarray(inputs["L1_attn_rg_gate_w"].T),
        "rggb": np.ascontiguousarray(inputs["L1_attn_rg_gate_b"].reshape(12, 128).T),
        "lamr": np.ascontiguousarray(inputs["L1_attn_forget_lambda"].reshape(12, 128).T),
        "n1w": np.ascontiguousarray(inputs["L1_attn_norm1_w"].reshape(8, 128).T),
        "lgbit": np.ascontiguousarray(
            np.stack([inputs["L1_ffn_ln_gamma"].reshape(8, 128).T,
                      inputs["L1_ffn_ln_beta"].reshape(8, 128).T],
                     axis=2).reshape(128, 16)),
    }
    return [dict(base, x_ext=host_l1_xext(xout_list, c)) for c in range(NCORES)]


def build_l1b():
    p = P("l1b")
    p.warm()
    x1 = p.inp("x1", [1024, T])
    xn1 = p.inp("xn1_bf", [1024, T], BF16)
    gg = p.inp("gategelu_bf", [1536, T], BF16)
    hrec = p.inp("hrec", [1536, T])
    cumP = p.inp("cumP", [1536, T])
    hfin = p.inp("hfin", [128, 12])
    xbit = p.inp("xbit_bf", [1024, T], BF16)
    maxes = p.inp("maxes", [1, 8])
    woutT = p.inp("woutT", [1536, 1024])
    n2w = p.inp("n2w", [128, 8])
    mgT = p.inp("mgT", [1024, 1536])
    muT = p.inp("muT", [1024, 1536])
    mdT = p.inp("mdT", [1536, 1024])
    bitw = p.inp("bitw", [1024, 1024])

    o_x2 = p.out("x2", [1024, T])

    d_xmid = p.dram("d_xmid", [1024, T])
    d_s = p.dram("d_s", [1024, T])

    ps_m = p.psum([128, 512], F32, "ps_m")
    ps_g2 = p.psum([128, 512], F32, "ps_g2")
    ps_r1 = p.psum([128, 512], F32, "ps_r1")

    onesf = p.sb2([128, 8], F32, "onesf")
    p.memset(onesf[:], 1.0)
    xs = p.sb2([128, 512], F32, "xs")
    xsq = p.sb2([128, 512], F32, "xsq")
    bc1 = p.sb2([128, 512], F32, "bc1")
    ev = p.sb2([128, 512], F32, "ev")
    ev2 = p.sb2([128, 512], F32, "ev2")
    evb = p.sb2([128, 512], BF16, "evb")
    rows_t = p.sb2([1, 8 * 512], F32, "rows_t")
    hfs = p.sb2([128, 12], F32, "hfs")
    n2s = p.sb2([128, 8], F32, "n2s")
    glob = p.sb2([1, 8], F32, "glob")
    p.dma(hfs[:], hfin)
    p.dma(n2s[:], n2w)

    def r_(i):
        return rows_t[0:1, i * 512:(i + 1) * 512]

    n2b = p.sb2([128, 8 * T], BF16, "n2b")

    wwo = p.sb2([128, 12 * 1024], BF16, "wwo")
    mixb = p.sb2([128, 12 * 512], BF16, "mixb")
    for kc in range(12):
        p.cast_dma(wwo[:, kc * 1024:(kc + 1) * 1024],
                   woutT[kc * 128:(kc + 1) * 128, :])
    for it in range(T // 512):
        c0 = it * 512
        for gc in range(12):
            p.dma(xs[:], hrec[gc * 128:(gc + 1) * 128, c0:c0 + 512])
            p.dma(xsq[:], cumP[gc * 128:(gc + 1) * 128, c0:c0 + 512])
            p.stt(xs[:], xsq[:], hfs[:, gc:gc + 1], xs[:], ALU.mult, ALU.add)
            p.dma(evb[:], gg[gc * 128:(gc + 1) * 128, c0:c0 + 512])
            p.cp(ev[:], evb[:])
            p.tt(xs[:], xs[:], ev[:], ALU.mult)
            p.cp(mixb[:, gc * 512:(gc + 1) * 512], xs[:])
        for oc in range(8):
            for kc in range(12):
                p.mm(ps_m[0:128, 0:512],
                     wwo[:, kc * 1024 + oc * 128:kc * 1024 + (oc + 1) * 128],
                     mixb[:, kc * 512:(kc + 1) * 512],
                     start=(kc == 0), stop=(kc == 11))
            p.dma(evb[:], xn1[oc * 128:(oc + 1) * 128, c0:c0 + 512])
            p.cp(ev[:], evb[:])
            p.tt(ev[:], ev[:], ps_m[0:128, 0:512], ALU.add)
            p.dma(d_xmid[oc * 128:(oc + 1) * 128, c0:c0 + 512], ev[:])
            p.act(xsq[:], ev[:], AF.Square)
            p.mm(ps_r1[0:1, 0:512], onesf[:, 0:1], xsq[:],
                 start=(oc == 0), stop=(oc == 7))
        p.cp(r_(0), ps_r1[0:1, 0:512])
        p.ts(r_(0), r_(0), 1.0 / D, 1e-5, ALU.mult, ALU.add)
        p.act(r_(0), r_(0), AF.Sqrt)
        p.recip(r_(0), r_(0))
        p.bcast(bc1[:], r_(0), 128, 512)
        for oc in range(8):
            p.dma(ev[:], d_xmid[oc * 128:(oc + 1) * 128, c0:c0 + 512])
            p.tt(ev[:], ev[:], bc1[:], ALU.mult)
            p.ts(ev[:], ev[:], n2s[:, oc:oc + 1], None, ALU.mult)
            p.cp(n2b[:, oc * T + c0:oc * T + c0 + 512], ev[:])
    p.sb_free("mixb", "wwo")

    wmg = p.sb2([128, 8 * 1536], BF16, "wmg")
    wmu = p.sb2([128, 8 * 1536], BF16, "wmu")
    wmd = p.sb2([128, 12 * 1024], BF16, "wmd")
    prodb = p.sb2([128, 12 * 512], BF16, "prodb")
    for kc in range(8):
        p.cast_dma(wmg[:, kc * 1536:(kc + 1) * 1536],
                   mgT[kc * 128:(kc + 1) * 128, :])
        p.cast_dma(wmu[:, kc * 1536:(kc + 1) * 1536],
                   muT[kc * 128:(kc + 1) * 128, :])
    for kc in range(12):
        p.cast_dma(wmd[:, kc * 1024:(kc + 1) * 1024],
                   mdT[kc * 128:(kc + 1) * 128, :])
    for it in range(T // 512):
        c0 = it * 512
        for gc in range(12):
            for kc in range(8):
                p.mm(ps_m[0:128, 0:512],
                     wmg[:, kc * 1536 + gc * 128:kc * 1536 + (gc + 1) * 128],
                     n2b[:, kc * T + c0:kc * T + c0 + 512],
                     start=(kc == 0), stop=(kc == 7))
            p.act(ev[:], ps_m[0:128, 0:512], AF.Gelu)
            for kc in range(8):
                p.mm(ps_g2[0:128, 0:512],
                     wmu[:, kc * 1536 + gc * 128:kc * 1536 + (gc + 1) * 128],
                     n2b[:, kc * T + c0:kc * T + c0 + 512],
                     start=(kc == 0), stop=(kc == 7))
            p.tt(ev[:], ev[:], ps_g2[0:128, 0:512], ALU.mult)
            p.cp(prodb[:, gc * 512:(gc + 1) * 512], ev[:])
        for oc in range(8):
            for kc in range(12):
                p.mm(ps_m[0:128, 0:512],
                     wmd[:, kc * 1024 + oc * 128:kc * 1024 + (oc + 1) * 128],
                     prodb[:, kc * 512:(kc + 1) * 512],
                     start=(kc == 0), stop=(kc == 11))
            p.dma(ev[:], d_xmid[oc * 128:(oc + 1) * 128, c0:c0 + 512])
            p.tt(ev[:], ev[:], ps_m[0:128, 0:512], ALU.add)
            p.dma(d_s[oc * 128:(oc + 1) * 128, c0:c0 + 512], ev[:])
    p.sb_free("prodb", "wmd", "wmu", "wmg")

    p.dma(glob[0:1, 0:8], maxes)
    p.red(glob[0:1, 0:1], glob[0:1, 0:8], ALU.max)
    wqb = p.sb2([128, 8 * 1024], BF16, "wqb")
    thr = p.sb2([128, 1], F32, "thr")
    s128 = p.sb2([128, 1], F32, "s128")
    sfin = p.sb2([128, 1], F32, "sfin")
    for kc in range(8):
        p.dma(xs[:, 0:512], bitw[kc * 128:(kc + 1) * 128, 0:512])
        p.dma(xsq[:, 0:512], bitw[kc * 128:(kc + 1) * 128, 512:1024])
        p.act(ev[:, 0:512], xs[:, 0:512], AF.Abs)
        p.act(ev2[:, 0:512], xsq[:, 0:512], AF.Abs)
        p.mm(ps_r1[0:1, 0:512], onesf[:, 0:1], ev[:, 0:512],
             start=(kc == 0), stop=False)
        p.mm(ps_r1[0:1, 0:512], onesf[:, 1:2], ev2[:, 0:512],
             start=False, stop=(kc == 7))
    p.cp(r_(0), ps_r1[0:1, 0:512])
    p.red(glob[0:1, 1:2], r_(0), ALU.add)
    p.ts(glob[0:1, 1:2], glob[0:1, 1:2], 1.0 / (1024.0 * 1024.0))
    p.ts(glob[0:1, 2:3], glob[0:1, 1:2], 1e-5, None, ALU.add)
    p.ts(glob[0:1, 2:3], glob[0:1, 2:3], 0.5, None, ALU.mult)
    p.recip(glob[0:1, 3:4], glob[0:1, 0:1])
    p.ts(glob[0:1, 3:4], glob[0:1, 3:4], 128.0, None, ALU.mult)
    p.tt(glob[0:1, 4:5], glob[0:1, 0:1], glob[0:1, 1:2], ALU.mult)
    p.ts(glob[0:1, 4:5], glob[0:1, 4:5], 1.0 / 128.0, None, ALU.mult)
    p.bcast(thr[:, 0:1], glob[0:1, 2:3], 128, 1)
    p.bcast(s128[:, 0:1], glob[0:1, 3:4], 128, 1)
    p.bcast(sfin[:, 0:1], glob[0:1, 4:5], 128, 1)
    for kc in range(8):
        for half in range(2):
            p.dma(xs[:, 0:512], bitw[kc * 128:(kc + 1) * 128,
                                     half * 512:(half + 1) * 512])
            p.ts(ev[:, 0:512], xs[:, 0:512], thr[:, 0:1], None, ALU.is_gt)
            p.ts(xsq[:, 0:512], xs[:, 0:512], -1.0, None, ALU.mult)
            p.ts(ev2[:, 0:512], xsq[:, 0:512], thr[:, 0:1], None, ALU.is_gt)
            p.tt(ev[:, 0:512], ev[:, 0:512], ev2[:, 0:512], ALU.subtract)
            p.cp(wqb[:, kc * 1024 + half * 512:kc * 1024 + (half + 1) * 512],
                 ev[:, 0:512])
    xqb = p.sb2([128, 8 * 512], BF16, "xqb")
    for it in range(T // 512):
        c0 = it * 512
        for kc in range(8):
            p.dma(evb[:], xbit[kc * 128:(kc + 1) * 128, c0:c0 + 512])
            p.cp(ev[:], evb[:])
            p.ts(ev[:], ev[:], s128[:, 0:1], None, ALU.mult)
            p.ts(ev[:], ev[:], 128.0 - 1e-5, None, ALU.min)
            p.ts(ev[:], ev[:], -(128.0 - 1e-5), None, ALU.max)
            p.cp(xqb[:, kc * 512:(kc + 1) * 512], ev[:])
        for oc in range(8):
            for kc in range(8):
                p.mm(ps_m[0:128, 0:512],
                     wqb[:, kc * 1024 + oc * 128:kc * 1024 + (oc + 1) * 128],
                     xqb[:, kc * 512:(kc + 1) * 512],
                     start=(kc == 0), stop=(kc == 7))
            p.ts(ev[:], ps_m[0:128, 0:512], sfin[:, 0:1], None, ALU.mult)
            p.act(ev[:], ev[:], AF.Silu)
            p.dma(ev2[:], d_s[oc * 128:(oc + 1) * 128, c0:c0 + 512])
            p.tt(ev[:], ev[:], ev2[:], ALU.mult)
            p.dma(ev2[:], x1[oc * 128:(oc + 1) * 128, c0:c0 + 512])
            p.tt(ev[:], ev[:], ev2[:], ALU.add)
            p.dma(o_x2[oc * 128:(oc + 1) * 128, c0:c0 + 512], ev[:])
    return p


def host_l1b(inputs, x1_list, a1):
    maxes = np.concatenate([a1[c]["locmax"].reshape(1) for c in range(NCORES)]
                           ).reshape(1, 8).astype(np.float32)
    base = {
        "maxes": maxes,
        "woutT": np.ascontiguousarray(inputs["L1_attn_W_out"].T),
        "n2w": np.ascontiguousarray(inputs["L1_attn_norm2_w"].reshape(8, 128).T),
        "mgT": np.ascontiguousarray(inputs["L1_attn_mlp_gate_w"].T),
        "muT": np.ascontiguousarray(inputs["L1_attn_mlp_up_w"].T),
        "mdT": np.ascontiguousarray(inputs["L1_attn_mlp_down_w"].T),
        "bitw": np.ascontiguousarray(inputs["L1_ffn_weight"]),
    }
    ims = []
    for c in range(NCORES):
        hh = c % 2
        hfin = a1[c ^ 1]["hfin"] if hh == 1 else np.zeros((128, 12), np.float32)
        im = dict(base, x1=x1_list[c], xn1_bf=a1[c]["xn1_bf"],
                  gategelu_bf=a1[c]["gategelu_bf"], hrec=a1[c]["hrec"],
                  cumP=a1[c]["cumP"], hfin=np.ascontiguousarray(hfin),
                  xbit_bf=a1[c]["xbit_bf"])
        ims.append(im)
    return ims


def emit_fp4(p, out_bf, xin, t1, t2, t3, bm, sc, W):
    G = W // 16
    v3 = lambda a: a.rearrange("p (g k) -> p g k", k=16)
    rep = lambda a: a.unsqueeze(2).broadcast_to([128, G, 16])
    p.act(t1, xin, AF.Abs)
    p.red(bm.unsqueeze(2), v3(t1), ALU.max)
    p.ts(sc, bm, 1.0 / 6.0, 1e-12, ALU.mult, ALU.max)
    p.recip(bm, sc)
    p.tt(v3(t1), v3(t1), rep(bm), ALU.mult)
    p.ts(t2, t1, 5.0, None, ALU.is_gt)
    p.stt(t2, t1, 3.5, t2, ALU.is_gt, ALU.add)
    p.stt(t2, t1, 2.5, t2, ALU.is_gt, ALU.add)
    p.stt(t2, t1, 5.0, t2, ALU.is_gt, ALU.add)
    p.ts(t3, t1, 0.25, None, ALU.is_gt)
    p.stt(t3, t1, 0.75, t3, ALU.is_gt, ALU.add)
    p.stt(t3, t1, 1.25, t3, ALU.is_gt, ALU.add)
    p.stt(t3, t1, 1.75, t3, ALU.is_gt, ALU.add)
    p.stt(t3, t2, 2.0, t3, ALU.mult, ALU.add)
    p.act(t1, xin, AF.Sign)
    p.ts(t1, t1, 0.5, None, ALU.mult)
    p.tt(v3(t1), v3(t1), rep(sc), ALU.mult)
    p.tt(t2, t3, t1, ALU.mult)
    p.cp(out_bf, t2)


def build_l2a():
    p = P("l2a")
    p.warm()
    x_ext = p.inp("x_ext", [1024, TE])
    cw4 = p.inp("cw4", [1024, 4])
    cb4 = p.inp("cb4", [1024, 1])
    xprojT = p.inp("xprojT", [1024, 33])
    dtw = p.inp("dtw", [128, 8])
    dtb2 = p.inp("dtb2", [128, 8])
    alg = p.inp("alg", [128, 128])

    o_xn = p.out("xn_bf", [1024, T], BF16)
    o_xc = p.out("xc_bf", [1024, T], BF16)
    o_proj = p.out("proj", [33, T])
    o_sfin = p.out("sfin", [128, 128])

    d_xn = p.dram("d_xn", [1024, TE])

    ps_m = p.psum([128, 512], F32, "ps_m")
    ps_r1 = p.psum([128, 512], F32, "ps_r1")
    ps_r2 = p.psum([128, 512], F32, "ps_r2")

    onesf = p.sb2([128, 8], F32, "onesf")
    p.memset(onesf[:], 1.0)
    xs = p.sb2([128, 512], F32, "xs")
    xsq = p.sb2([128, 512], F32, "xsq")
    bc1 = p.sb2([128, 512], F32, "bc1")
    bc2 = p.sb2([128, 512], F32, "bc2")
    ev = p.sb2([128, 512], F32, "ev")
    evb = p.sb2([128, 512], BF16, "evb")
    rows_t = p.sb2([1, 8 * 512], F32, "rows_t")
    cws = p.sb2([128, 8 * 4], F32, "cws")
    cbs = p.sb2([128, 8], F32, "cbs")
    dtws = p.sb2([128, 8], F32, "dtws")
    dtbs = p.sb2([128, 8], F32, "dtbs")
    algs = p.sb2([128, 128], F32, "algs")
    wxp = p.sb2([128, 8 * 33], BF16, "wxp")
    p.dma(dtws[:], dtw)
    p.dma(dtbs[:], dtb2)
    p.dma(algs[:], alg)
    p.act(algs[:], algs[:], AF.Exp)
    p.ts(algs[:], algs[:], -1.0)
    for kc in range(8):
        p.dma(cws[:, kc * 4:(kc + 1) * 4], cw4[kc * 128:(kc + 1) * 128, :])
        p.dma(cbs[:, kc:kc + 1], cb4[kc * 128:(kc + 1) * 128, :])
        p.cast_dma(wxp[:, kc * 33:(kc + 1) * 33], xprojT[kc * 128:(kc + 1) * 128, :])

    def r_(i):
        return rows_t[0:1, i * 512:(i + 1) * 512]

    for it in range(TE // 512):
        c0 = it * 512
        for kc in range(8):
            p.dma(xs[:], x_ext[kc * 128:(kc + 1) * 128, c0:c0 + 512])
            p.act(xsq[:], xs[:], AF.Square)
            p.mm(ps_r1[0:1, 0:512], onesf[:, 0:1], xs[:],
                 start=(kc == 0), stop=(kc == 7))
            p.mm(ps_r2[0:1, 0:512], onesf[:, 1:2], xsq[:],
                 start=(kc == 0), stop=(kc == 7))
        p.cp(r_(0), ps_r1[0:1, 0:512])
        p.cp(r_(1), ps_r2[0:1, 0:512])
        p.ts(r_(2), r_(0), 1.0 / D)
        p.act(r_(4), r_(2), AF.Square)
        p.ts(r_(5), r_(1), 1.0 / D)
        p.tt(r_(5), r_(5), r_(4), ALU.subtract)
        p.ts(r_(5), r_(5), 1e-5, None, ALU.add)
        p.act(r_(5), r_(5), AF.Sqrt)
        p.recip(r_(3), r_(5))
        p.bcast(bc1[:], r_(2), 128, 512)
        p.bcast(bc2[:], r_(3), 128, 512)
        incore = c0 + 512 > HB and c0 < HB + T
        lo = max(HB, c0) - c0
        hi = min(HB + T, c0 + 512) - c0
        for kc in range(8):
            p.dma(xs[:], x_ext[kc * 128:(kc + 1) * 128, c0:c0 + 512])
            p.tt(xsq[:], xs[:], bc1[:], ALU.subtract)
            p.tt(xsq[:], xsq[:], bc2[:], ALU.mult)
            p.dma(d_xn[kc * 128:(kc + 1) * 128, c0:c0 + 512], xsq[:])
            if incore:
                p.cp(evb[:], xsq[:])
                p.dma(o_xn[kc * 128:(kc + 1) * 128, c0 - HB + lo:c0 - HB + hi],
                      evb[:, lo:hi])
    xw = p.sb2([128, 516], F32, "xw")
    ca = p.sb2([128, 512], F32, "ca")
    xcb = p.sb2([128, 8 * 512], BF16, "xcb")
    for it in range(T // 512):
        c0 = HB + it * 512
        for kc in range(8):
            p.dma(xw[:, 0:515], d_xn[kc * 128:(kc + 1) * 128, c0 - 3:c0 + 512])
            p.ts(ca[:], xw[:, 0:512], cws[:, kc * 4:kc * 4 + 1], None, ALU.mult)
            for j in range(1, 4):
                p.stt(ca[:], xw[:, j:j + 512], cws[:, kc * 4 + j:kc * 4 + j + 1],
                      ca[:], ALU.mult, ALU.add)
            p.act(ca[:], ca[:], AF.Silu, bias=cbs[:, kc:kc + 1])
            p.cp(xcb[:, kc * 512:(kc + 1) * 512], ca[:])
            p.dma(o_xc[kc * 128:(kc + 1) * 128, it * 512:(it + 1) * 512],
                  xcb[:, kc * 512:(kc + 1) * 512])
        for kc in range(8):
            p.mm(ps_m[0:33, 0:512], wxp[:, kc * 33:(kc + 1) * 33],
                 xcb[:, kc * 512:(kc + 1) * 512],
                 start=(kc == 0), stop=(kc == 7))
        p.cp(ev[0:33, :], ps_m[0:33, 0:512])
        p.dma(o_proj[:, it * 512:(it + 1) * 512], ev[0:33, :])
    p.sb_free("xcb", "ca", "xw")

    bcb = p.sb2([128, T], F32, "bcb")
    dl = p.sb2([128, T], F32, "dl")
    dxc = p.sb2([128, T], F32, "dxc")
    ab = p.sb2([128, T], F32, "ab")
    un = p.sb2([128, T], F32, "un")
    hh = p.sb2([128, T], F32, "hh")
    sfs = p.sb2([128, 128], F32, "sfs")
    xcl = p.sb2([128, T], BF16, "xcl")
    for dc in range(8):
        p.dma(bcb[:], o_proj[32:33, :].broadcast_to([128, T]))
        p.ts(dl[:], bcb[:], dtws[:, dc:dc + 1], dtbs[:, dc:dc + 1],
             ALU.mult, ALU.add)
        p.softplus(un[:], dl[:], ab[:])
        p.cp(dl[:], un[:])
        p.dma(xcl[:], o_xc[dc * 128:(dc + 1) * 128, :])
        p.cp(dxc[:], xcl[:])
        p.tt(dxc[:], dxc[:], dl[:], ALU.mult)
        for n in range(16):
            p.ts(ab[:], dl[:], algs[:, dc * 16 + n:dc * 16 + n + 1], None, ALU.mult)
            p.act(ab[:], ab[:], AF.Exp)
            p.dma(bcb[:], o_proj[n:n + 1, :].broadcast_to([128, T]))
            p.tt(un[:], dxc[:], bcb[:], ALU.mult)
            p.scan(hh[:], ab[:], un[:], 0.0, ALU.mult, ALU.add)
            p.cp(sfs[:, dc * 16 + n:dc * 16 + n + 1], hh[:, T - 1:T])
    p.dma(o_sfin[:], sfs[:])
    return p


def host_l2a(inputs, xout_list):
    alg = inputs["L2_attn_a_log"].reshape(8, 128, 16).transpose(1, 0, 2)
    base = {
        "cw4": np.ascontiguousarray(inputs["L2_attn_conv_w"][:, 0, :]),
        "cb4": np.ascontiguousarray(inputs["L2_attn_conv_b"][:, None]),
        "xprojT": np.ascontiguousarray(inputs["L2_attn_xproj_w"].T),
        "dtw": np.ascontiguousarray(inputs["L2_attn_dt_w"].reshape(8, 128).T),
        "dtb2": np.ascontiguousarray(inputs["L2_attn_dt_b"].reshape(8, 128).T),
        "alg": np.ascontiguousarray(alg.reshape(128, 128)),
    }
    return [dict(base, x_ext=host_l1_xext(xout_list, c)) for c in range(NCORES)]


def build_l2b():
    p = P("l2b")
    p.warm()
    x2 = p.inp("x2", [1024, T])
    xn_bf = p.inp("xn_bf", [1024, T], BF16)
    xc_bf = p.inp("xc_bf", [1024, T], BF16)
    proj = p.inp("proj", [33, T])
    sinit = p.inp("sinit", [128, 128])
    dtw = p.inp("dtw", [128, 8])
    dtb2 = p.inp("dtb2", [128, 8])
    alg = p.inp("alg", [128, 128])
    dpw = p.inp("dpw", [128, 8])
    wgate = p.inp("wgate", [2816, 1024])
    wup = p.inp("wup", [2816, 1024])
    wdown = p.inp("wdown", [1024, 2816])

    o_x3 = p.out("x3", [1024, T])

    d_s = p.dram("d_s", [1024, T])
    d_xq = p.dram("d_xq", [T, 1024], BF16)
    d_wqg = p.dram("d_wqg", [2816, 1024], BF16)
    d_wqu = p.dram("d_wqu", [2816, 1024], BF16)
    d_wqd = p.dram("d_wqd", [1024, 2816], BF16)
    d_h = p.dram("d_h", [2816, T], BF16)
    d_hq = p.dram("d_hq", [T, 2816], BF16)

    ps_m = p.psum([128, 512], F32, "ps_m")
    ps_g2 = p.psum([128, 512], F32, "ps_g2")

    dtws = p.sb2([128, 8], F32, "dtws")
    dtbs = p.sb2([128, 8], F32, "dtbs")
    algs = p.sb2([128, 128], F32, "algs")
    dps = p.sb2([128, 8], F32, "dps")
    sis = p.sb2([128, 128], F32, "sis")
    p.dma(dtws[:], dtw)
    p.dma(dtbs[:], dtb2)
    p.dma(algs[:], alg)
    p.act(algs[:], algs[:], AF.Exp)
    p.ts(algs[:], algs[:], -1.0)
    p.dma(dps[:], dpw)
    p.dma(sis[:], sinit)

    bcb = p.sb2([128, T], F32, "bcb")
    dl = p.sb2([128, T], F32, "dl")
    dxc = p.sb2([128, T], F32, "dxc")
    ab = p.sb2([128, T], F32, "ab")
    un = p.sb2([128, T], F32, "un")
    hh = p.sb2([128, T], F32, "hh")
    ya = p.sb2([128, T], F32, "ya")
    xcl = p.sb2([128, T], BF16, "xcl")
    for dc in range(8):
        p.dma(bcb[:], proj[32:33, :].broadcast_to([128, T]))
        p.ts(dl[:], bcb[:], dtws[:, dc:dc + 1], dtbs[:, dc:dc + 1],
             ALU.mult, ALU.add)
        p.softplus(un[:], dl[:], ab[:])
        p.cp(dl[:], un[:])
        p.dma(xcl[:], xc_bf[dc * 128:(dc + 1) * 128, :])
        p.cp(dxc[:], xcl[:])
        p.tt(dxc[:], dxc[:], dl[:], ALU.mult)
        p.memset(ya[:], 0.0)
        for n in range(16):
            p.ts(ab[:], dl[:], algs[:, dc * 16 + n:dc * 16 + n + 1], None, ALU.mult)
            p.act(ab[:], ab[:], AF.Exp)
            p.dma(bcb[:], proj[n:n + 1, :].broadcast_to([128, T]))
            p.tt(un[:], dxc[:], bcb[:], ALU.mult)
            p.scan(hh[:], ab[:], un[:], sis[:, dc * 16 + n:dc * 16 + n + 1],
                   ALU.mult, ALU.add)
            p.dma(bcb[:], proj[16 + n:16 + n + 1, :].broadcast_to([128, T]))
            p.tt(hh[:], hh[:], bcb[:], ALU.mult)
            p.tt(ya[:], ya[:], hh[:], ALU.add)
        p.dma(xcl[:], xn_bf[dc * 128:(dc + 1) * 128, :])
        p.cp(un[:], xcl[:])
        p.stt(ya[:], un[:], dps[:, dc:dc + 1], ya[:], ALU.mult, ALU.add)
        p.dma(d_s[dc * 128:(dc + 1) * 128, :], ya[:])
    p.sb_free("xcl", "ya", "hh", "un", "ab", "dxc", "dl", "bcb")

    qa = p.sb2([128, 2816], F32, "qa")
    qb = p.sb2([128, 2816], F32, "qb")
    qc = p.sb2([128, 2816], F32, "qc")
    qx = p.sb2([128, 2816], F32, "qx")
    qbm = p.sb2([128, 176], F32, "qbm")
    qsc = p.sb2([128, 176], F32, "qsc")
    qob = p.sb2([128, 2816], BF16, "qob")
    for (win, dout, nr, W) in ((wgate, d_wqg, 22, 1024), (wup, d_wqu, 22, 1024),
                               (wdown, d_wqd, 8, 2816)):
        for rc in range(nr):
            p.dma(qx[:, 0:W], win[rc * 128:(rc + 1) * 128, :])
            emit_fp4(p, qob[:, 0:W], qx[:, 0:W], qa[:, 0:W], qb[:, 0:W],
                     qc[:, 0:W], qbm[:, 0:W // 16], qsc[:, 0:W // 16], W)
            p.dma(dout[rc * 128:(rc + 1) * 128, :], qob[:, 0:W])
    qxb = p.sb2([128, 1024], BF16, "qxb")
    for tc in range(32):
        p.dmat(qxb[:, 0:1024], xn_bf[:, tc * 128:(tc + 1) * 128])
        p.cp(qx[:, 0:1024], qxb[:, 0:1024])
        emit_fp4(p, qob[:, 0:1024], qx[:, 0:1024], qa[:, 0:1024], qb[:, 0:1024],
                 qc[:, 0:1024], qbm[:, 0:64], qsc[:, 0:64], 1024)
        p.dma(d_xq[tc * 128:(tc + 1) * 128, :], qob[:, 0:1024])
    p.sb_free("qxb")

    wgT_sb = p.sb2([128, 8 * 2816], BF16, "wgT_sb")
    wuT_sb = p.sb2([128, 8 * 2816], BF16, "wuT_sb")
    xqf = p.sb2([128, 8 * 512], BF16, "xqf")
    hb = p.sb2([128, 512], BF16, "hb")
    for kc in range(8):
        for mc in range(22):
            p.dmat(wgT_sb[:, kc * 2816 + mc * 128:kc * 2816 + (mc + 1) * 128],
                   d_wqg[mc * 128:(mc + 1) * 128, kc * 128:(kc + 1) * 128])
            p.dmat(wuT_sb[:, kc * 2816 + mc * 128:kc * 2816 + (mc + 1) * 128],
                   d_wqu[mc * 128:(mc + 1) * 128, kc * 128:(kc + 1) * 128])
    for it in range(T // 512):
        c0 = it * 512
        for kc in range(8):
            p.dmat(xqf[:, kc * 512:(kc + 1) * 512],
                   d_xq[c0:c0 + 512, kc * 128:(kc + 1) * 128])
        for mc in range(22):
            for kc in range(8):
                p.mm(ps_m[0:128, 0:512],
                     wgT_sb[:, kc * 2816 + mc * 128:kc * 2816 + (mc + 1) * 128],
                     xqf[:, kc * 512:(kc + 1) * 512],
                     start=(kc == 0), stop=(kc == 7))
            p.act(qa[:, 0:512], ps_m[0:128, 0:512], AF.Silu)
            for kc in range(8):
                p.mm(ps_g2[0:128, 0:512],
                     wuT_sb[:, kc * 2816 + mc * 128:kc * 2816 + (mc + 1) * 128],
                     xqf[:, kc * 512:(kc + 1) * 512],
                     start=(kc == 0), stop=(kc == 7))
            p.tt(qa[:, 0:512], qa[:, 0:512], ps_g2[0:128, 0:512], ALU.mult)
            p.cp(hb[:], qa[:, 0:512])
            p.dma(d_h[mc * 128:(mc + 1) * 128, c0:c0 + 512], hb[:])
    p.sb_free("hb", "xqf", "wuT_sb", "wgT_sb")
    qhb = p.sb2([128, 2816], BF16, "qhb")
    for tc in range(32):
        p.dmat(qhb[:, 0:2816], d_h[:, tc * 128:(tc + 1) * 128])
        p.cp(qx[:, 0:2816], qhb[:, 0:2816])
        emit_fp4(p, qob[:, 0:2816], qx[:, 0:2816], qa[:, 0:2816], qb[:, 0:2816],
                 qc[:, 0:2816], qbm[:, 0:176], qsc[:, 0:176], 2816)
        p.dma(d_hq[tc * 128:(tc + 1) * 128, :], qob[:, 0:2816])
    p.sb_free("qhb")
    wdT_sb = p.sb2([128, 22 * 1024], BF16, "wdT_sb")
    hqf = p.sb2([128, 22 * 512], BF16, "hqf")
    ev = p.sb2([128, 512], F32, "ev")
    ev2 = p.sb2([128, 512], F32, "ev2")
    for kc in range(22):
        for mc in range(8):
            p.dmat(wdT_sb[:, kc * 1024 + mc * 128:kc * 1024 + (mc + 1) * 128],
                   d_wqd[mc * 128:(mc + 1) * 128, kc * 128:(kc + 1) * 128])
    for it in range(T // 512):
        c0 = it * 512
        for kc in range(22):
            p.dmat(hqf[:, kc * 512:(kc + 1) * 512],
                   d_hq[c0:c0 + 512, kc * 128:(kc + 1) * 128])
        for mc in range(8):
            for kc in range(22):
                p.mm(ps_m[0:128, 0:512],
                     wdT_sb[:, kc * 1024 + mc * 128:kc * 1024 + (mc + 1) * 128],
                     hqf[:, kc * 512:(kc + 1) * 512],
                     start=(kc == 0), stop=(kc == 21))
            p.act(ev[:], ps_m[0:128, 0:512], AF.Silu)
            p.dma(ev2[:], d_s[mc * 128:(mc + 1) * 128, c0:c0 + 512])
            p.tt(ev[:], ev[:], ev2[:], ALU.mult)
            p.dma(ev2[:], x2[mc * 128:(mc + 1) * 128, c0:c0 + 512])
            p.tt(ev[:], ev[:], ev2[:], ALU.add)
            p.dma(o_x3[mc * 128:(mc + 1) * 128, c0:c0 + 512], ev[:])
    return p


def host_l2b(inputs, x2_list, a2):
    base = {
        "dtw": np.ascontiguousarray(inputs["L2_attn_dt_w"].reshape(8, 128).T),
        "dtb2": np.ascontiguousarray(inputs["L2_attn_dt_b"].reshape(8, 128).T),
        "alg": np.ascontiguousarray(
            inputs["L2_attn_a_log"].reshape(8, 128, 16).transpose(1, 0, 2)
            .reshape(128, 128)),
        "dpw": np.ascontiguousarray(inputs["L2_attn_Dp"].reshape(8, 128).T),
        "wgate": np.ascontiguousarray(inputs["L2_ffn_w_gate"]),
        "wup": np.ascontiguousarray(inputs["L2_ffn_w_up"]),
        "wdown": np.ascontiguousarray(inputs["L2_ffn_w_down"]),
    }
    ims = []
    for c in range(NCORES):
        hh = c % 2
        sinit = a2[c ^ 1]["sfin"] if hh == 1 else np.zeros((128, 128), np.float32)
        ims.append(dict(base, x2=x2_list[c], xn_bf=a2[c]["xn_bf"],
                        xc_bf=a2[c]["xc_bf"], proj=a2[c]["proj"],
                        sinit=np.ascontiguousarray(sinit)))
    return ims


def build_l3a():
    p = P("l3a")
    p.warm()
    x3 = p.inp("x3", [1024, T])
    wqT = p.inp("wqT", [1024, 1024])
    bq = p.inp("bq", [128, 8])
    wkT = p.inp("wkT", [1024, 1024])
    bk = p.inp("bk", [128, 8])
    wvT = p.inp("wvT", [1024, 1024])
    bv = p.inp("bv", [128, 8])
    wbT = p.inp("wbT", [1024, 1024])
    bb = p.inp("bb", [128, 8])
    waT = p.inp("waT", [1024, 1024])
    ba = p.inp("ba", [128, 8])
    fqT = p.inp("fqT", [1024, 1024])
    fkT = p.inp("fkT", [1024, 1024])
    vwT = p.inp("vwT", [1024, 8192])
    selvT = p.inp("selvT", [1024, 64])
    seloT = p.inp("seloT", [1024, 64])
    rsc = p.inp("rsc", [1, 1])
    emat = p.inp("emat", [1024, 16])

    o_xn = p.out("xn3_bf", [1024, T], BF16)
    o_q3 = p.out("q3_bf", [1024, T], BF16)
    o_k3 = p.out("k3_bf", [1024, T], BF16)
    o_vb = p.out("vbeta_bf", [1024, T], BF16)
    o_vt2 = p.out("vt2_bf", [1024, T], BF16)
    o_m = p.out("m3", [16, T])
    o_asum = p.out("asum", [16, NCH])
    o_sf = p.out("sfin3", [64, 1024])
    o_fq = p.out("fq_bf", [1024, T], BF16)
    o_fk = p.out("fk_bf", [1024, T], BF16)
    o_vau = p.out("vaug_bf", [T, 1032], BF16)
    o_go = p.out("goT", [64, T])

    d_beta = p.dram("d_beta", [1024, T], BF16)
    d_la = p.dram("d_la", [1024, T])
    d_bm = p.dram("d_bm", [16, T])
    d_cc = p.dram("d_cc", [1024, T])

    ps_m = p.psum([128, 512], F32, "ps_m")
    ps_r1 = p.psum([128, 512], F32, "ps_r1")
    ps_r2 = p.psum([128, 512], F32, "ps_r2")
    ps_e = p.psum([16, 512], F32, "ps_e")

    onesf = p.sb2([128, 8], F32, "onesf")
    p.memset(onesf[:], 1.0)
    xs = p.sb2([128, 512], F32, "xs")
    xsq = p.sb2([128, 512], F32, "xsq")
    bc1 = p.sb2([128, 512], F32, "bc1")
    bc2 = p.sb2([128, 512], F32, "bc2")
    ev = p.sb2([128, 512], F32, "ev")
    ev2 = p.sb2([128, 512], F32, "ev2")
    ev3 = p.sb2([128, 512], F32, "ev3")
    evb = p.sb2([128, 512], BF16, "evb")
    rows_t = p.sb2([1, 8 * 512], F32, "rows_t")
    xnb = p.sb2([128, 8 * T], BF16, "xnb")
    ems = p.sb2([128, 8 * 16], F32, "ems")
    for dc in range(8):
        p.dma(ems[:, dc * 16:(dc + 1) * 16], emat[dc * 128:(dc + 1) * 128, :])

    def r_(i):
        return rows_t[0:1, i * 512:(i + 1) * 512]

    for it in range(T // 512):
        c0 = it * 512
        for kc in range(8):
            p.dma(xs[:], x3[kc * 128:(kc + 1) * 128, c0:c0 + 512])
            p.act(xsq[:], xs[:], AF.Square)
            p.mm(ps_r1[0:1, 0:512], onesf[:, 0:1], xs[:],
                 start=(kc == 0), stop=(kc == 7))
            p.mm(ps_r2[0:1, 0:512], onesf[:, 1:2], xsq[:],
                 start=(kc == 0), stop=(kc == 7))
        p.cp(r_(0), ps_r1[0:1, 0:512])
        p.cp(r_(1), ps_r2[0:1, 0:512])
        p.ts(r_(2), r_(0), 1.0 / D)
        p.act(r_(4), r_(2), AF.Square)
        p.ts(r_(5), r_(1), 1.0 / D)
        p.tt(r_(5), r_(5), r_(4), ALU.subtract)
        p.ts(r_(5), r_(5), 1e-5, None, ALU.add)
        p.act(r_(5), r_(5), AF.Sqrt)
        p.recip(r_(3), r_(5))
        p.bcast(bc1[:], r_(2), 128, 512)
        p.bcast(bc2[:], r_(3), 128, 512)
        for kc in range(8):
            p.dma(xs[:], x3[kc * 128:(kc + 1) * 128, c0:c0 + 512])
            p.tt(xsq[:], xs[:], bc1[:], ALU.subtract)
            p.tt(xsq[:], xsq[:], bc2[:], ALU.mult)
            p.cp(xnb[:, kc * T + c0:kc * T + c0 + 512], xsq[:])
            p.dma(o_xn[kc * 128:(kc + 1) * 128, c0:c0 + 512],
                  xnb[:, kc * T + c0:kc * T + c0 + 512])

    bqs = p.sb2([128, 8], F32, "bqs")
    wt = p.sb2([128, 8 * 1024], BF16, "wt")

    def proj_pass(wT, bias_sb, epil):
        for kc in range(8):
            p.cast_dma(wt[:, kc * 1024:(kc + 1) * 1024],
                       wT[kc * 128:(kc + 1) * 128, :])
        for it in range(T // 512):
            c0 = it * 512
            for mc in range(8):
                for kc in range(8):
                    p.mm(ps_m[0:128, 0:512],
                         wt[:, kc * 1024 + mc * 128:kc * 1024 + (mc + 1) * 128],
                         xnb[:, kc * T + c0:kc * T + c0 + 512],
                         start=(kc == 0), stop=(kc == 7))
                epil(mc, it, ps_m[0:128, 0:512])

    p.dma(bqs[:], bb)
    def ep_beta(mc, it, ps):
        c0 = it * 512
        p.act(ev[:], ps, AF.Identity, bias=bqs[:, mc:mc + 1])
        p.softplus(ev2[:], ev[:], ev3[:])
        p.cp(evb[:], ev2[:])
        p.dma(d_beta[mc * 128:(mc + 1) * 128, c0:c0 + 512], evb[:])
        p.mm(ps_e[0:16, 0:512], ems[:, mc * 16:(mc + 1) * 16], ev2[:],
             start=(mc == 0), stop=(mc == 7))
        if mc == 7:
            p.cp(ev[0:16, :], ps_e[0:16, 0:512])
            p.dma(d_bm[:, c0:c0 + 512], ev[0:16, :])
    proj_pass(wbT, bqs, ep_beta)
    p.dma(bqs[:], ba)
    def ep_alpha(mc, it, ps):
        c0 = it * 512
        p.act(ev[:], ps, AF.Identity, bias=bqs[:, mc:mc + 1])
        p.ts(ev[:], ev[:], 10.0, None, ALU.min)
        p.ts(ev[:], ev[:], -10.0, None, ALU.max)
        p.ts(ev[:], ev[:], -1.0, None, ALU.mult)
        p.softplus(ev2[:], ev[:], ev3[:])
        p.ts(ev2[:], ev2[:], -1.0, None, ALU.mult)
        p.dma(d_la[mc * 128:(mc + 1) * 128, c0:c0 + 512], ev2[:])
    proj_pass(waT, bqs, ep_alpha)
    p.dma(bqs[:], bv)
    def ep_v(mc, it, ps):
        c0 = it * 512
        p.act(ev[:], ps, AF.Identity, bias=bqs[:, mc:mc + 1])
        p.dma(evb[:], d_beta[mc * 128:(mc + 1) * 128, c0:c0 + 512])
        p.cp(ev2[:], evb[:])
        p.tt(ev2[:], ev2[:], ev[:], ALU.mult)
        p.cp(evb[:], ev2[:])
        p.dma(o_vb[mc * 128:(mc + 1) * 128, c0:c0 + 512], evb[:])
        p.dma(bc1[0:64, :], d_bm[2 * mc:2 * mc + 1, c0:c0 + 512]
              .broadcast_to([64, 512]))
        p.dma(bc1[64:128, :], d_bm[2 * mc + 1:2 * mc + 2, c0:c0 + 512]
              .broadcast_to([64, 512]))
        p.tt(ev[:], ev[:], bc1[:], ALU.mult)
        p.cp(evb[:], ev[:])
        p.dma(o_vt2[mc * 128:(mc + 1) * 128, c0:c0 + 512], evb[:])
    proj_pass(wvT, bqs, ep_v)
    for wT, bsrc, dst in ((wqT, bq, o_q3), (wkT, bk, o_k3)):
        p.dma(bqs[:], bsrc)
        def ep_qk(mc, it, ps, dst=dst):
            c0 = it * 512
            p.act(ev[:], ps, AF.Identity, bias=bqs[:, mc:mc + 1])
            p.ts(ev[:], ev[:], 0.125, None, ALU.mult)
            p.cp(evb[:], ev[:])
            p.dma(dst[mc * 128:(mc + 1) * 128, c0:c0 + 512], evb[:])
        proj_pass(wT, bqs, ep_qk)
    SC3 = float(1.0 / np.sqrt(128.0))
    def ep_fq(mc, it, ps):
        c0 = it * 512
        p.ts(ev[:], ps, SC3, None, ALU.mult)
        p.cp(evb[:], ev[:])
        p.dma(o_fq[mc * 128:(mc + 1) * 128, c0:c0 + 512], evb[:])
    proj_pass(fqT, None, ep_fq)
    def ep_fk(mc, it, ps):
        c0 = it * 512
        p.cp(evb[:], ps)
        p.dma(o_fk[mc * 128:(mc + 1) * 128, c0:c0 + 512], evb[:])
    proj_pass(fkT, None, ep_fk)
    p.sb_free("wt")

    wsel = p.sb2([128, 8 * 128], BF16, "wsel")
    gvt = p.sb2([128, 64], F32, "gvt")
    got = p.sb2([128, 64], F32, "got")
    rs_s = p.sb2([128, 1], F32, "rs_s")
    glob = p.sb2([1, 8], F32, "glob")
    idt3 = p.sb2([128, 128], F32, "idt3")
    ident3 = p.inp("ident", [128, 128])
    p.dma(idt3[:], ident3)
    p.dma(glob[0:1, 0:1], rsc)
    p.bcast(rs_s[:, 0:1], glob[0:1, 0:1], 128, 1)
    for kc in range(8):
        p.cast_dma(wsel[:, kc * 128:kc * 128 + 64], selvT[kc * 128:(kc + 1) * 128, :])
        p.cast_dma(wsel[:, kc * 128 + 64:kc * 128 + 128],
                   seloT[kc * 128:(kc + 1) * 128, :])
    d_sel = p.dram("d_sel", [128, T], BF16)
    d_gv = p.dram("d_gv", [T, 64])
    for it in range(T // 512):
        c0 = it * 512
        for half in range(2):
            for kc in range(8):
                p.mm(ps_m[0:64, 0:512],
                     wsel[:, kc * 128 + half * 64:kc * 128 + half * 64 + 64],
                     xnb[:, kc * T + c0:kc * T + c0 + 512],
                     start=(kc == 0), stop=(kc == 7))
            p.cp(evb[0:64, :], ps_m[0:64, 0:512])
            p.dma(d_sel[half * 64:half * 64 + 64, c0:c0 + 512], evb[0:64, :])
    v3g = lambda a: a.rearrange("p (g k) -> p g k", k=8)
    rep8 = lambda a: a.unsqueeze(2).broadcast_to([128, 8, 8])
    selt = p.sb2([128, 64], BF16, "selt")
    pr = p.sb2([128, 64], F32, "pr")
    m1 = p.sb2([128, 8], F32, "m1")
    m2 = p.sb2([128, 8], F32, "m2")
    msk = p.sb2([128, 64], F32, "msk")
    for tc in range(32):
        cc0 = tc * 128
        for half, dst in ((0, gvt), (1, got)):
            p.dmat(selt[:, 0:64], d_sel[half * 64:half * 64 + 64, cc0:cc0 + 128])
            p.cp(pr[:], selt[:, 0:64])
            p.act(pr[:], pr[:], AF.Sigmoid)
            p.red(m1[:].unsqueeze(2), v3g(pr[:]), ALU.max)
            p.tt(msk[:].rearrange("p (g k) -> p g k", k=8), v3g(pr[:]),
                 rep8(m1[:]), ALU.is_ge)
            p.stt(msk[:], msk[:], -1e9, pr[:], ALU.mult, ALU.add)
            p.red(m2[:].unsqueeze(2), v3g(msk[:]), ALU.max)
            p.tt(msk[:].rearrange("p (g k) -> p g k", k=8), v3g(pr[:]),
                 rep8(m2[:]), ALU.is_ge)
            p.tt(pr[:], pr[:], msk[:], ALU.mult)
            p.red(m1[:].unsqueeze(2), v3g(pr[:]), ALU.add)
            p.ts(m1[:], m1[:], 1e-9, None, ALU.max)
            p.recip(m1[:], m1[:])
            p.tt(pr[:].rearrange("p (g k) -> p g k", k=8), v3g(pr[:]),
                 rep8(m1[:]), ALU.mult)
            p.ts(dst[:], pr[:], rs_s[:, 0:1], None, ALU.mult)
        p.dma(d_gv[cc0:cc0 + 128, :], gvt[:])
        p.transp(ps_m[0:64, 0:128], got[:], idt3[:])
        p.cp(ev[0:64, 0:128], ps_m[0:64, 0:128])
        p.dma(o_go[:, cc0:cc0 + 128], ev[0:64, 0:128])
    p.sb_free("msk", "m2", "m1", "pr", "selt")

    wv_sb = p.sb2([128, 8 * 1024], BF16, "wv_sb")
    gvs = p.sb2([128, 8], F32, "gvs")
    vh = p.sb2([128, 128], F32, "vh")
    vhb = p.sb2([128, 129], BF16, "vhb")
    for h in range(8):
        for kc in range(8):
            p.cast_dma(wv_sb[:, kc * 1024:(kc + 1) * 1024],
                       vwT[kc * 128:(kc + 1) * 128, h * 1024:(h + 1) * 1024])
        for tc in range(32):
            cc0 = tc * 128
            p.dma(gvs[:], d_gv[cc0:cc0 + 128, h * 8:(h + 1) * 8])
            p.memset(vhb[:], 1.0)
            for half in range(2):
                for kc in range(8):
                    p.mm(ps_m[0:128, 0:512],
                         xnb[:, kc * T + cc0:kc * T + cc0 + 128],
                         wv_sb[:, kc * 1024 + half * 512:kc * 1024 + (half + 1) * 512],
                         start=(kc == 0), stop=(kc == 7))
                for j in range(4):
                    e = half * 4 + j
                    if e == 0:
                        p.ts(vh[:], ps_m[0:128, j * 128:(j + 1) * 128],
                             gvs[:, e:e + 1], None, ALU.mult)
                    else:
                        p.stt(vh[:], ps_m[0:128, j * 128:(j + 1) * 128],
                              gvs[:, e:e + 1], vh[:], ALU.mult, ALU.add)
            p.cp(vhb[:, 0:128], vh[:])
            p.dma(o_vau[cc0:cc0 + 128, h * 129:(h + 1) * 129], vhb[:])
    p.sb_free("vhb", "vh", "gvs", "wv_sb")

    la = p.sb2([128, T], F32, "la")
    for dc in range(8):
        p.dma(la[:], d_la[dc * 128:(dc + 1) * 128, :])
        for c in range(NCH):
            sl = slice(c * Q, (c + 1) * Q)
            p.scan(la[:, sl], la[:, sl], la[:, sl], 0.0, ALU.add, ALU.bypass)
        p.dma(d_cc[dc * 128:(dc + 1) * 128, :], la[:])
    mrow = p.sb2([16, T], F32, "mrow")
    for it in range(T // 512):
        c0 = it * 512
        for dc in range(8):
            p.dma(la[:, 0:512], d_cc[dc * 128:(dc + 1) * 128, c0:c0 + 512])
            p.mm(ps_e[0:16, 0:512], ems[:, dc * 16:(dc + 1) * 16], la[:, 0:512],
                 start=(dc == 0), stop=(dc == 7))
        p.cp(mrow[:, c0:c0 + 512], ps_e[0:16, 0:512])
    p.dma(o_m[:], mrow[:])
    asr = p.sb2([16, NCH], F32, "asr")
    p.cp(asr[:], mrow[:, Q - 1::Q])
    p.act(asr[:], asr[:], AF.Exp)
    p.dma(o_asum[:], asr[:])
    p.sb_free("asr", "mrow", "la")

    d_asf = p.dram("d_asf", [1, 512])
    ksm = p.sb2([128, 1024], BF16, "ksm")
    vsm = p.sb2([128, 1024], BF16, "vsm")
    s3 = p.sb2([64, 1024], F32, "s3")
    lee3 = p.sb2([64, 512], F32, "lee3")
    ps_b3 = p.ps2([64, 1024], F32, "ps_b3")
    p.dma(d_asf[:], o_asum.flatten().unsqueeze(0))
    p.dma(lee3[:], d_asf.broadcast_to([64, 512]))
    p.memset(s3[:], 0.0)
    for c in range(NCH):
        sl = slice(c * Q, (c + 1) * Q)
        p.dmat(ksm[:], o_k3[:, sl])
        p.dmat(vsm[:], o_vt2[:, sl])
        for h in range(16):
            p.mm(ps_b3[0:64, h * 64:(h + 1) * 64], ksm[:, h * 64:(h + 1) * 64],
                 vsm[:, h * 64:(h + 1) * 64], start=True, stop=True)
        leap = lee3[:, c:c + 15 * 32 + 1:32].unsqueeze(2).broadcast_to([64, 16, 64])
        p.tt(s3[:].rearrange("p (a k) -> p a k", k=64),
             s3[:].rearrange("p (a k) -> p a k", k=64), leap, ALU.mult)
        p.tt(s3[:], s3[:], ps_b3[0:64, :], ALU.add)
    p.dma(o_sf[:], s3[:])
    return p


def host_l3a(inputs, xout_list):
    emat = np.zeros((1024, 16), np.float32)
    for h in range(16):
        emat[h * 64:(h + 1) * 64, h] = 1.0 / 64.0
    base = {
        "wqT": np.ascontiguousarray(inputs["L3_attn_Wq"].T),
        "bq": np.ascontiguousarray(inputs["L3_attn_bq"].reshape(8, 128).T),
        "wkT": np.ascontiguousarray(inputs["L3_attn_Wk"].T),
        "bk": np.ascontiguousarray(inputs["L3_attn_bk"].reshape(8, 128).T),
        "wvT": np.ascontiguousarray(inputs["L3_attn_Wv"].T),
        "bv": np.ascontiguousarray(inputs["L3_attn_bv"].reshape(8, 128).T),
        "wbT": np.ascontiguousarray(inputs["L3_attn_Wbeta"].T),
        "bb": np.ascontiguousarray(inputs["L3_attn_bbeta"].reshape(8, 128).T),
        "waT": np.ascontiguousarray(inputs["L3_attn_Walpha"].T),
        "ba": np.ascontiguousarray(inputs["L3_attn_balpha"].reshape(8, 128).T),
        "fqT": np.ascontiguousarray(inputs["L3_ffn_q_w"].T),
        "fkT": np.ascontiguousarray(inputs["L3_ffn_k_w"].T),
        "vwT": np.ascontiguousarray(
            inputs["L3_ffn_v_w"].transpose(1, 0, 2).reshape(1024, 8192)),
        "selvT": np.ascontiguousarray(inputs["L3_ffn_sel_v_w"].T),
        "seloT": np.ascontiguousarray(inputs["L3_ffn_sel_o_w"].T),
        "rsc": np.ascontiguousarray(inputs["L3_ffn_route_scale"].reshape(1, 1)),
        "emat": emat,
        "ident": np.eye(128, dtype=np.float32),
    }
    return [dict(base, x3=xout_list[c]) for c in range(NCORES)]


def build_l3b():
    p = P("l3b")
    p.warm()
    x3 = p.inp("x3", [1024, T])
    xn3 = p.inp("xn3_bf", [1024, T], BF16)
    q3 = p.inp("q3_bf", [1024, T], BF16)
    k3 = p.inp("k3_bf", [1024, T], BF16)
    vbeta = p.inp("vbeta_bf", [1024, T], BF16)
    vt2 = p.inp("vt2_bf", [1024, T], BF16)
    m3 = p.inp("m3", [16, T])
    asum = p.inp("asum", [16, NCH])
    goT = p.inp("goT", [64, T])
    fq = p.inp("fq_bf", [1024, T], BF16)
    kfull = p.inp("kfull", [1024, 2 * T], BF16)
    vaufull = p.inp("vaufull", [2 * T, 1032], BF16)
    sinit3 = p.inp("sinit3", [64, 1024])
    oww = p.inp("oww", [8192, 1024])
    maskf = p.inp("maskf", [128, 128])
    ident = p.inp("ident", [128, 128])

    o_x4 = p.out("x4", [1024, T])

    d_y3 = p.dram("d_y3", [1024, T])
    d_expm = p.dram("d_expm", [16, T])
    d_ctx = p.dram("d_ctx", [1024, T], BF16)
    d_asf = p.dram("d_asf", [1, 512])

    ev = p.sb2([128, 512], F32, "ev")
    ev2 = p.sb2([128, 512], F32, "ev2")
    evb = p.sb2([128, 512], BF16, "evb")
    idt = p.sb2([128, 128], F32, "idt")
    p.dma(idt[:], ident)
    mr = p.sb2([16, T], F32, "mr")
    p.dma(mr[:], m3)
    for it in range(T // 512):
        p.act(ev[0:16, 0:512], mr[:, it * 512:(it + 1) * 512], AF.Exp)
        p.dma(d_expm[:, it * 512:(it + 1) * 512], ev[0:16, 0:512])

    ps_m = p.ps2([128, 512], F32, "ps_m")
    ps_g = p.ps2([128, 128], F32, "ps_g")
    ps_y = p.ps2([64, 128], F32, "ps_y")
    ps_b3 = p.ps2([64, 1024], F32, "ps_b3")
    mskt = p.sb2([128, 128], F32, "mskt")
    p.dma(mskt[:], maskf)
    tst = p.sb2([128, 128], F32, "tst")
    bcl3 = p.sb2([128, 2048], F32, "bcl3")
    m3s = p.sb2([128, 2048], F32, "m3s")
    mhb = p.sb2([128, 128], BF16, "mhb")
    gsb = p.sb2([128, 128], F32, "gsb")
    msm = p.sb2([128, 16], F32, "msm")
    ebc3 = p.sb2([64, 2048], F32, "ebc3")
    kcs = p.sb2([128, 8 * 128], BF16, "kcs")
    qcs = p.sb2([128, 8 * 128], BF16, "qcs")
    vbs = p.sb2([128, 1024], BF16, "vbs")
    ksm = p.sb2([128, 1024], BF16, "ksm")
    vsm = p.sb2([128, 1024], BF16, "vsm")
    s3 = p.sb2([64, 1024], F32, "s3")
    s3b = p.sb2([64, 1024], BF16, "s3b")
    lee3 = p.sb2([64, 512], F32, "lee3")
    ysb = p.sb2([64, 128], F32, "ysb")
    qtmp = p.sb2([64, 128], BF16, "qtmp")
    p.dma(d_asf[:], asum.flatten().unsqueeze(0))
    p.dma(lee3[:], d_asf.broadcast_to([64, 512]))
    p.dma(s3[:], sinit3)
    v16 = lambda a: a.rearrange("p (a k) -> p a k", k=Q)
    for c in range(NCH):
        sl = slice(c * Q, (c + 1) * Q)
        p.dma(bcl3[:].rearrange("p (a k) -> p a k", k=Q),
              m3.rearrange("a b -> a b")[:, sl].unsqueeze(0)
              .broadcast_to([128, 16, Q]))
        p.memset(tst[:], 0.0)
        p.cp(tst[0:16, 0:Q], mr[:, sl])
        p.transp(ps_m[0:128, 0:16], tst[0:16, :] if False else tst[:, :],
                 idt[:]) if False else p.transp(ps_m[0:128, 0:128], tst[:], idt[:])
        p.cp(msm[:], ps_m[0:128, 0:16])
        msr = msm[:].unsqueeze(2).broadcast_to([128, 16, Q])
        p.tt(v16(m3s[:]), v16(bcl3[:]), msr, ALU.subtract)
        mrep = mskt[:].unsqueeze(1).broadcast_to([128, 16, Q])
        p.tt(v16(m3s[:]), v16(m3s[:]), mrep, ALU.add)
        p.act(m3s[:], m3s[:], AF.Exp)
        for kc in range(8):
            p.dma(kcs[:, kc * 128:(kc + 1) * 128], k3[kc * 128:(kc + 1) * 128, sl])
            p.dma(qcs[:, kc * 128:(kc + 1) * 128], q3[kc * 128:(kc + 1) * 128, sl])
        p.dmat(vbs[:], vbeta[:, sl])
        p.cp(s3b[:], s3[:])
        p.dma(ebc3[:].rearrange("p (a k) -> p a k", k=Q),
              d_expm[:, sl].unsqueeze(0).broadcast_to([64, 16, Q]))
        for h in range(16):
            hc, hb_ = h // 2, (h % 2) * 64
            p.mm(ps_g[0:128, 0:128], kcs[hb_:hb_ + 64, hc * 128:(hc + 1) * 128],
                 qcs[hb_:hb_ + 64, hc * 128:(hc + 1) * 128], start=True, stop=True)
            p.cp(gsb[:], ps_g[0:128, 0:128])
            p.tt(gsb[:], gsb[:], m3s[:, h * Q:(h + 1) * Q], ALU.mult)
            p.cp(mhb[:], gsb[:])
            p.mm(ps_y[0:64, 0:Q], vbs[:, h * 64:(h + 1) * 64], mhb[:],
                 start=True, stop=False)
            p.cp(qtmp[:], qcs[hb_:hb_ + 64, hc * 128:(hc + 1) * 128])
            p.mm(ps_y[0:64, 0:Q], s3b[:, h * 64:(h + 1) * 64],
                 qtmp[:], start=False, stop=True)
            p.tt(ysb[:], ps_y[0:64, 0:Q], ebc3[:, h * Q:(h + 1) * Q], ALU.mult)
            p.dma(d_y3[h * 64:(h + 1) * 64, sl], ysb[:])
        p.dmat(ksm[:], k3[:, sl])
        p.dmat(vsm[:], vt2[:, sl])
        for h in range(16):
            p.mm(ps_b3[0:64, h * 64:(h + 1) * 64], ksm[:, h * 64:(h + 1) * 64],
                 vsm[:, h * 64:(h + 1) * 64], start=True, stop=True)
        leap = lee3[:, c:c + 15 * 32 + 1:32].unsqueeze(2).broadcast_to([64, 16, 64])
        p.tt(s3[:].rearrange("p (a k) -> p a k", k=64),
             s3[:].rearrange("p (a k) -> p a k", k=64), leap, ALU.mult)
        p.tt(s3[:], s3[:], ps_b3[0:64, :], ALU.add)
    p.sb_free("qtmp", "ysb", "lee3", "s3b", "s3", "vsm", "ksm", "vbs", "qcs", "kcs",
              "ebc3", "msm", "gsb", "mhb", "m3s", "bcl3", "tst", "mskt",
              "ps_b3", "ps_y", "ps_g", "ps_m")

    ps4 = p.ps2([128, 2048], F32, "ps4")
    ctxps = p.ps2([128, 512], F32, "ctxps")
    psrow = p.ps2([1, 512], F32, "psrow")
    onesb = p.sb2([128, 1], BF16, "onesb")
    p.memset(onesb[:], 1.0)
    kfh = p.sb2([128, 2 * T], BF16, "kfh")
    vauh = p.sb2([128, 64 * 129], BF16, "vauh")
    qfh = p.sb2([128, T], BF16, "qfh")
    pb = p.sb2([128, 2048], BF16, "pb")
    rowr = p.sb2([1, 512], F32, "rowr")
    bcr = p.sb2([128, 512], F32, "bcr")
    for h in range(8):
        p.dma(kfh[:], kfull[h * 128:(h + 1) * 128, :])
        p.dma(vauh[:].rearrange("p (a k) -> p a k", k=129),
              vaufull[:, h * 129:(h + 1) * 129]
              .rearrange("(a p) k -> p a k", p=128))
        p.dma(qfh[:], fq[h * 128:(h + 1) * 128, :])
        for qb in range(T // 512):
            q0 = qb * 512
            for kg in range(16):
                for j in range(4):
                    kc = kg * 4 + j
                    p.mm(ps4[0:128, j * 512:(j + 1) * 512],
                         kfh[:, kc * 128:(kc + 1) * 128], qfh[:, q0:q0 + 512],
                         start=True, stop=True)
                p.act(pb[:], ps4[0:128, 0:2048], AF.Exp)
                for j in range(4):
                    kc = kg * 4 + j
                    p.mm(psrow[0:1, 0:512], onesb[:, 0:1],
                         pb[:, j * 512:(j + 1) * 512],
                         start=(kc == 0), stop=(kc == 63))
                    p.mm(ctxps[0:128, 0:512],
                         vauh[:, kc * 129:kc * 129 + 128],
                         pb[:, j * 512:(j + 1) * 512],
                         start=(kc == 0), stop=(kc == 63))
            p.cp(rowr[0:1, :], psrow[0:1, 0:512])
            p.recip(rowr[0:1, :], rowr[0:1, :])
            p.bcast(bcr[:], rowr[0:1, :], 128, 512)
            p.tt(ev[:], ctxps[0:128, 0:512], bcr[:], ALU.mult)
            p.cp(evb[:], ev[:])
            p.dma(d_ctx[h * 128:(h + 1) * 128, q0:q0 + 512], evb[:])
    p.sb_free("bcr", "rowr", "pb", "qfh", "vauh", "kfh", "onesb",
              "psrow", "ctxps", "ps4")

    pso = [p.ps2([128, 512], F32, f"pso{i}") for i in range(8)]
    ctxall = p.sb2([128, 8 * 512], BF16, "ctxall")
    wef = p.sb2([128, 8 * 512], BF16, "wef")
    ow_e = p.sb2([128, 8 * 1024], BF16, "ow_e")
    bcg = p.sb2([128, 512], BF16, "bcg")
    for qb in range(T // 512):
        q0 = qb * 512
        for h in range(8):
            p.dma(ctxall[:, h * 512:(h + 1) * 512],
                  d_ctx[h * 128:(h + 1) * 128, q0:q0 + 512])
        for e in range(8):
            for h in range(8):
                p.cast_dma(bcg[:], goT[h * 8 + e:h * 8 + e + 1, q0:q0 + 512]
                           .broadcast_to([128, 512]))
                p.tt(wef[:, h * 512:(h + 1) * 512],
                     ctxall[:, h * 512:(h + 1) * 512], bcg[:], ALU.mult)
                p.cast_dma(ow_e[:, h * 1024:(h + 1) * 1024],
                           oww[(h * 8 + e) * 128:(h * 8 + e + 1) * 128, :])
            for dc in range(8):
                for h in range(8):
                    p.mm(pso[dc][0:128, 0:512],
                         ow_e[:, h * 1024 + dc * 128:h * 1024 + (dc + 1) * 128],
                         wef[:, h * 512:(h + 1) * 512],
                         start=(e == 0 and h == 0), stop=(e == 7 and h == 7))
        for dc in range(8):
            p.act(ev[:], pso[dc][0:128, 0:512], AF.Silu)
            p.dma(ev2[:], d_y3[dc * 128:(dc + 1) * 128, q0:q0 + 512])
            p.tt(ev[:], ev[:], ev2[:], ALU.mult)
            p.dma(ev2[:], x3[dc * 128:(dc + 1) * 128, q0:q0 + 512])
            p.tt(ev[:], ev[:], ev2[:], ALU.add)
            p.dma(o_x4[dc * 128:(dc + 1) * 128, q0:q0 + 512], ev[:])
    return p


def host_l3b(inputs, xout_list, a3):
    base = {
        "oww": np.ascontiguousarray(inputs["L3_ffn_o_w"].reshape(8192, 1024)),
        "maskf": np.where(np.arange(128)[:, None] <= np.arange(128)[None, :],
                          0.0, -1e9).astype(np.float32),
        "ident": np.eye(128, dtype=np.float32),
    }
    ims = []
    for c in range(NCORES):
        hh = c % 2
        par = a3[c ^ 1]
        me = a3[c]
        if hh == 0:
            kfull = np.concatenate([me["fk_bf"], par["fk_bf"]], axis=1)
            vaufull = np.concatenate([me["vaug_bf"], par["vaug_bf"]], axis=0)
        else:
            kfull = np.concatenate([par["fk_bf"], me["fk_bf"]], axis=1)
            vaufull = np.concatenate([par["vaug_bf"], me["vaug_bf"]], axis=0)
        sinit3 = par["sfin3"] if hh == 1 else np.zeros((64, 1024), np.float32)
        ims.append(dict(base, x3=xout_list[c], xn3_bf=me["xn3_bf"],
                        q3_bf=me["q3_bf"], k3_bf=me["k3_bf"],
                        vbeta_bf=me["vbeta_bf"], vt2_bf=me["vt2_bf"],
                        m3=me["m3"], asum=me["asum"], goT=me["goT"],
                        fq_bf=me["fq_bf"],
                        kfull=np.ascontiguousarray(kfull),
                        vaufull=np.ascontiguousarray(vaufull),
                        sinit3=np.ascontiguousarray(sinit3)))
    return ims


def kernel(**inputs):
    inputs = {k: np.asarray(v) for k, v in inputs.items()}

    a0 = [dict(r) for r in run_prog(build_l0a, host_l0a(inputs)).results]
    b0 = run_prog(build_l0b, host_l0b(inputs, a0)).results
    xout = [np.asarray(r["xout"]) for r in b0]
    del a0, b0
    a1 = [dict(r) for r in run_prog(build_l1a, host_l1a(inputs, xout)).results]
    b1 = run_prog(build_l1b, host_l1b(inputs, xout, a1)).results
    xout = [np.asarray(r["x2"]) for r in b1]
    del a1, b1
    a2 = [dict(r) for r in run_prog(build_l2a, host_l2a(inputs, xout)).results]
    b2 = run_prog(build_l2b, host_l2b(inputs, xout, a2)).results
    xout = [np.asarray(r["x3"]) for r in b2]
    del a2, b2
    a3 = [dict(r) for r in run_prog(build_l3a, host_l3a(inputs, xout)).results]
    b3 = run_prog(build_l3b, host_l3b(inputs, xout, a3)).results
    x4 = [np.asarray(r["x4"]) for r in b3]

    out = np.empty((4, 8192, 1024), np.float32)
    for c in range(NCORES):
        b, hh = c // 2, c % 2
        out[b, hh * T:(hh + 1) * T, :] = x4[c].T
    return out
